# version 10
# baseline (speedup 1.0000x reference)
import contextlib
import numpy as np

import concourse.bass as bass
import concourse.mybir as mybir
from concourse.bass_utils import run_bass_kernel_spmd

F32 = mybir.dt.float32
BF16 = mybir.dt.bfloat16
F16 = mybir.dt.float16
AF = mybir.ActivationFunctionType
ALU = mybir.AluOpType
AX = mybir.AxisListType

M32 = 12582912.0
M64 = 6755399441055744.0
PI = 3.141592653589793
TWO_PI = 2.0 * PI
CLAMP = 3.14159

T = 2048
DM = 1024
DH = 4096
H = 8
GRP = 256
NG = T // GRP

ENG2SECT = {"pe": "tensor", "act": "scalar", "dve": "vector", "pool": "gpsimd", "sp": "sync"}
DMA_SEMS = frozenset(["d0", "d1", "d2", "d3", "ds", "wr", "wa", "wb"])


class Emit:

    def __init__(self, serialize=False):
        self.serialize = serialize
        self.last = None
        self.prog = {e: [] for e in ENG2SECT}
        self.cnt = {}
        self.seen = {e: {} for e in ENG2SECT}
        self.buf = {}
        self.pending = {e: [] for e in ENG2SECT}

    def op(self, eng, fn, reads=(), writes=(), sem=None, inc=1):
        semn = sem or eng
        self.cnt.setdefault(semn, 0)
        need = {}

        def dep(src):
            s, c = src
            if s in DMA_SEMS:
                c = self.cnt[s]
            if self.seen[eng].get(s, 0) >= c:
                return
            if need.get(s, 0) < c:
                need[s] = c

        for b in reads:
            r = self.buf.get(b)
            if r and r["w"]:
                dep(r["w"])
        for b in writes:
            r = self.buf.get(b)
            if r:
                if r["w"]:
                    dep(r["w"])
                for s, c in r["rs"].items():
                    dep((s, c))
        if self.serialize and self.last is not None:
            dep(self.last)
        waits = sorted(need.items())
        for s, c in waits:
            self.seen[eng][s] = c
        if inc:
            self.cnt[semn] += inc
            mycnt = (semn, self.cnt[semn])
        else:
            mycnt = None

        def rec(b, as_write, cntv):
            r = self.buf.setdefault(b, {"w": None, "rs": {}})
            if as_write:
                r["w"] = cntv
                r["rs"] = {}
            else:
                s, c = cntv
                if r["rs"].get(s, 0) < c:
                    r["rs"][s] = c

        if mycnt is None:
            for b in reads:
                self.pending[eng].append((b, False))
            for b in writes:
                self.pending[eng].append((b, True))
        else:
            for b, as_w in self.pending[eng]:
                rec(b, as_w, mycnt)
            self.pending[eng] = []
            for b in reads:
                rec(b, False, mycnt)
            for b in writes:
                rec(b, True, mycnt)
        if inc:
            self.last = mycnt
        self.prog[eng].append((waits, fn, semn if inc else None, inc, mycnt))

    def join(self, bufs):
        for b in bufs:
            r = self.buf.get(b)
            if r and r["w"]:
                s, _ = r["w"]
                r["w"] = (s, self.cnt[s])

    def barrier(self):
        snap = [(s, c) for s, c in self.cnt.items() if c > 0]
        for e in ENG2SECT:
            todo = [(s, c) for s, c in snap if self.seen[e].get(s, 0) < c]
            for s, c in todo:
                self.seen[e][s] = c
            if todo:
                self.prog[e].append((todo, None, None, 0, None))

    def finalize_check(self):
        for e, p in self.pending.items():
            assert not p, f"pending unresolved on {e}: {p[:4]}"
        for e in ENG2SECT:
            for waits, _, _, _, _ in self.prog[e]:
                for s, c in waits:
                    assert c <= self.cnt.get(s, 0), f"wait {s}>={c} never reached"


def build(upto="all", serialize=False):
    nc = bass.Bass("TRN2", target_bir_lowering=False)
    d_x = nc.dram_tensor("x", [DM, T], F32, kind="ExternalInput").ap()
    d_wq = nc.dram_tensor("wq", [DM, DM], F32, kind="ExternalInput").ap()
    d_wk = nc.dram_tensor("wk", [DM, DM], F32, kind="ExternalInput").ap()
    d_wv = nc.dram_tensor("wv", [DM, DM], F32, kind="ExternalInput").ap()
    d_wo = nc.dram_tensor("wo", [DM, DM], F32, kind="ExternalInput").ap()
    d_w1t = nc.dram_tensor("w1t", [DM, DH], F32, kind="ExternalInput").ap()
    d_w2t = nc.dram_tensor("w2t", [DH, DM], F32, kind="ExternalInput").ap()
    d_cw = nc.dram_tensor("cw", [128, 96], F32, kind="ExternalInput").ap()
    d_cb = nc.dram_tensor("cb", [128, 32], F32, kind="ExternalInput").ap()
    d_out = nc.dram_tensor("out", [DM, T], F32, kind="ExternalOutput").ap()
    d_xwork = nc.dram_tensor("xwork", [DM, T], F32, kind="Internal").ap()
    d_w1q = nc.dram_tensor("w1q", [DM, DH], BF16, kind="Internal").ap()
    d_w2q = nc.dram_tensor("w2q", [DH, DM], BF16, kind="Internal").ap()

    em = Emit(serialize=serialize)
    O = em.op
    _uid = [0]

    def _un(n):
        _uid[0] += 1
        return f"{n}_{_uid[0]}"

    with contextlib.ExitStack() as top:
        sb = lambda n, sh, dt: top.enter_context(nc.sbuf_tensor(n, sh, dt))
        ones_bf = sb("ones_bf", [128, 128], BF16)
        ones_f = sb("ones_f", [128, 128], F32)
        onesrow = sb("onesrow", [1, 128], F32)
        c127row = sb("c127row", [1, 128], F32)
        onescol_bf = sb("onescol_bf", [128, 1], BF16)
        identf = sb("identf", [128, 128], F32)
        identb = sb("identb", [128, 128], BF16)
        masktb = sb("masktb", [128, 128], BF16)
        selE = sb("selE", [128, 128], BF16)
        selO = sb("selO", [128, 128], BF16)
        selscr = sb("selscr", [128, 128], F32)
        iot = sb("iot", [128, 128], F32)
        wcol = sb("wcol", [128, 1], F32)
        mcol = sb("mcol", [128, 1], F32)
        nmcol = sb("nmcol", [128, 1], F32)
        eps5 = sb("eps5", [128, 1], F32)
        eps6 = sb("eps6", [128, 1], F32)
        c128 = sb("c128", [128, 1], F32)
        m1col = sb("m1col", [128, 1], F32)
        lncol = sb("lncol", [1, 1], F32)
        phc = sb("phc", [128, 1], F32)
        phs = sb("phs", [128, 1], F32)
        phcL = sb("phcL", [128, 1], F32)
        phsL = sb("phsL", [128, 1], F32)
        lnL = sb("lnL", [1, 1], F32)
        mwc = sb("mwc", [128, 4], F32)
        mwa = sb("mwa", [128, 2], F32)
        cwS = sb("cwS", [128, 96], F32)
        cbS = sb("cbS", [128, 32], F32)
        rowA = sb("rowA", [1, 512], F32)
        rowB = sb("rowB", [1, 512], F32)
        rowC = sb("rowC", [1, 512], F32)
        rowD = sb("rowD", [1, 512], F32)
        rowE = sb("rowE", [1, 512], F32)
        freq2 = sb("freq2", [1, 128], F32)

        psA = top.enter_context(nc.psum_tensor("psA", [128, 512], F32))
        psB = top.enter_context(nc.psum_tensor("psB", [128, 512], F32))
        psC = top.enter_context(nc.psum_tensor("psC", [128, 512], F32))
        psD = top.enter_context(nc.psum_tensor("psD", [128, 512], F32))
        psE = top.enter_context(nc.psum_tensor("psE", [128, 512], F32))
        psF = top.enter_context(nc.psum_tensor("psF", [128, 512], F32))
        psS = top.enter_context(nc.psum_tensor("psS", [128, 384], F32))
        pst = top.enter_context(nc.psum_tensor("pst", [128, 128], BF16))

        semobjs = {}
        for s in ["pe", "act", "dve", "pool", "sp", "d0", "d1", "d2", "d3", "ds", "wr", "wa", "wb"]:
            semobjs[s] = top.enter_context(nc.semaphore("sem_" + s))
            em.cnt.setdefault(s, 0)

        O("dve", lambda e: e.memset(ones_bf[:], 1.0), writes=("ones_bf",))
        O("dve", lambda e: e.memset(ones_f[:], 1.0), writes=("ones_f",))
        O("dve", lambda e: e.memset(onesrow[:], 1.0), writes=("onesrow",))
        O("dve", lambda e: e.memset(c127row[:], 127.0), writes=("c127row",))
        O("dve", lambda e: e.memset(onescol_bf[:], 1.0), writes=("onescol",))
        O("dve", lambda e: e.memset(wcol[:], 1.0), writes=("wcol",))
        O("dve", lambda e: e.tensor_scalar(out=wcol[:], in0=wcol[:], scalar1=M64, scalar2=-M64, op0=ALU.add, op1=ALU.add), reads=("wcol",), writes=("wcol",))
        O("dve", lambda e: e.tensor_scalar(out=mcol[:], in0=wcol[:], scalar1=M64 - M32, scalar2=M32, op0=ALU.mult, op1=ALU.add), reads=("wcol",), writes=("mcol",))
        O("dve", lambda e: e.tensor_scalar(out=nmcol[:], in0=mcol[:], scalar1=-1.0, scalar2=None, op0=ALU.mult), reads=("mcol",), writes=("nmcol",))
        O("dve", lambda e: e.memset(eps5[:], 1e-5), writes=("eps5",))
        O("dve", lambda e: e.memset(eps6[:], 1e-6), writes=("eps6",))
        O("dve", lambda e: e.memset(c128[:], 1.0 / 128.0), writes=("c128",))
        O("dve", lambda e: e.memset(m1col[:], -1.0), writes=("m1col",))
        _LN = -float(np.log(np.float64(10000.0))) / 64.0
        _LNH = float(np.float32(_LN))
        _LNL = float(np.float64(_LN) - np.float64(_LNH))
        _PH = float(np.float32(PI / 2.0))
        _PL = float(np.float64(PI / 2.0) - np.float64(_PH))
        O("dve", lambda e: e.memset(lncol[:], _LNH), writes=("lncol",))
        O("dve", lambda e: e.memset(lnL[:], _LNL), writes=("lnL",))
        O("dve", lambda e: e.memset(phc[0:64, :], _PH), writes=("phc",))
        O("dve", lambda e: e.memset(phc[64:128, :], 0.0), reads=("phc",), writes=("phc",))
        O("dve", lambda e: e.memset(phcL[0:64, :], _PL), writes=("phcL",))
        O("dve", lambda e: e.memset(phcL[64:128, :], 0.0), reads=("phcL",), writes=("phcL",))
        O("dve", lambda e: e.memset(phs[0:64, :], 0.0), writes=("phs",))
        O("dve", lambda e: e.memset(phs[64:128, :], _PH), reads=("phs",), writes=("phs",))
        O("dve", lambda e: e.memset(phsL[0:64, :], 0.0), writes=("phsL",))
        O("dve", lambda e: e.memset(phsL[64:128, :], _PL), reads=("phsL",), writes=("phsL",))
        O("pool", lambda e: e.iota(out=iot[:], pattern=[[1, 128]], base=0, channel_multiplier=-1, allow_small_or_imprecise_dtypes=True), writes=("iot",))
        O("dve", lambda e: e.tensor_scalar(out=identf[:], in0=iot[:], scalar1=0.0, scalar2=None, op0=ALU.is_equal), reads=("iot",), writes=("identf",))
        O("dve", lambda e: e.tensor_copy(out=identb[:], in_=identf[:]), reads=("identf",), writes=("identb",))
        O("dve", lambda e: e.tensor_scalar(out=masktb[:], in0=iot[:], scalar1=0.0, scalar2=None, op0=ALU.is_ge), reads=("iot",), writes=("masktb",))
        O("dve", lambda e: e.memset(selE[:], 0.0), writes=("selE",))
        O("dve", lambda e: e.memset(selO[:], 0.0), writes=("selO",))
        O("dve", lambda e: e.tensor_scalar(out=selscr[:], in0=iot[:], scalar1=-64.0, scalar2=None, op0=ALU.is_equal), reads=("iot",), writes=("selscr",))
        O("dve", lambda e: e.tensor_tensor(out=selE[:, 0:64], in0=identf[:, 0:64], in1=selscr[:, 0:64], op=ALU.subtract), reads=("identf", "selscr", "selE"), writes=("selE",))
        O("dve", lambda e: e.tensor_scalar(out=selscr[:], in0=iot[:], scalar1=64.0, scalar2=None, op0=ALU.is_equal), reads=("iot", "selscr"), writes=("selscr",))
        O("dve", lambda e: e.tensor_tensor(out=selO[:, 64:128], in0=identf[:, 64:128], in1=selscr[:, 64:128], op=ALU.add), reads=("identf", "selscr", "selO"), writes=("selO",))
        O("pool", lambda e: e.iota(out=freq2[:], pattern=[[0, 2], [1, 64]], base=0, channel_multiplier=0, allow_small_or_imprecise_dtypes=True), writes=("freq2",))
        O("dve", lambda e: e.tensor_scalar(out=rowE[0:1, 0:128], in0=freq2[:], scalar1=lnL[:], scalar2=None, op0=ALU.mult), reads=("freq2", "lnL", "rowE"), writes=("rowE",))
        O("dve", lambda e: e.tensor_scalar(out=freq2[:], in0=freq2[:], scalar1=lncol[:], scalar2=None, op0=ALU.mult), reads=("freq2", "lncol"), writes=("freq2",))
        O("dve", lambda e: e.tensor_tensor(out=freq2[:], in0=freq2[:], in1=rowE[0:1, 0:128], op=ALU.add), reads=("freq2", "rowE"), writes=("freq2",))
        O("act", lambda e: e.activation(out=freq2[:], in_=freq2[:], func=AF.Exp), reads=("freq2",), writes=("freq2",))
        O("sp", lambda e: e.dma_start(out=cwS[:], in_=d_cw[:]), writes=("cwS",), sem="d2", inc=16)
        O("sp", lambda e: e.dma_start(out=cbS[:], in_=d_cb[:]), writes=("cbS",), sem="d2", inc=16)
        em.join(("cwS", "cbS"))

        def rne_round(src_ap, src_bufs, dst_ap, dst_buf, eng="dve"):
            O(eng, lambda e: e.tensor_scalar(out=dst_ap, in0=src_ap, scalar1=mcol[:], scalar2=nmcol[:], op0=ALU.add, op1=ALU.add), reads=tuple(src_bufs) + ("mcol", "nmcol", dst_buf), writes=(dst_buf,))

        def emit_W():
            with contextlib.ExitStack() as ws:
                wsb = lambda n, sh, dt: ws.enter_context(nc.sbuf_tensor(_un(n), sh, dt))
                wstr = [wsb(f"wstr{i}", [128, 2048], F32) for i in range(4)]
                wqb = [wsb("wqb0", [128, 2048], BF16), wsb("wqb1", [128, 2048], BF16)]
                wabs = wsb("wabs", [128, 2048], BF16)
                wcs = wsb("wcs", [128, 64], F32)
                wtT = [wsb("wtT0", [128, 2048], F32), wsb("wtT1", [128, 2048], F32)]

                pieces = []
                for c in range(8):
                    for hh in range(2):
                        pieces.append(("w1", c, hh, 2048))
                for c in range(32):
                    pieces.append(("w2", c, 0, DM))

                def src_ap(wn, c, hh, wid):
                    d = d_w1t if wn == "w1" else d_w2t
                    return d[c * 128:(c + 1) * 128, hh * 2048:hh * 2048 + wid]

                for ci, (wn, c, hh, wid) in enumerate(pieces):
                    p = ci % 4
                    O("sp", lambda e, wn=wn, c=c, hh=hh, wid=wid, p=p: e.dma_start(out=wstr[p][:, 0:wid], in_=src_ap(wn, c, hh, wid)), reads=(f"wstr{p}",), writes=(f"wstr{p}",), sem=f"d{p}", inc=16)
                    if ci % 2 == 0:
                        O("act", lambda e, p=p, wid=wid, ci=ci: e.activation(out=wabs[:, 0:wid], in_=wstr[p][:, 0:wid], func=AF.Abs, accum_out=wcs[:, ci:ci + 1]), reads=(f"wstr{p}", "wabs", "wcs"), writes=("wcs", "wabs"))
                    else:
                        O("dve", lambda e, p=p, wid=wid, ci=ci: e.tensor_reduce(out=wcs[:, ci:ci + 1], in_=wstr[p][:, 0:wid], axis=AX.X, op=ALU.add, apply_absolute_value=True), reads=(f"wstr{p}", "wcs"), writes=("wcs",))
                O("pe", lambda e: e.matmul(psS[:, 0:48], ones_f[:], wcs[:, 0:48], start=True, stop=True), reads=("wcs", "ones_f", "psSa"), writes=("psSa",))
                O("dve", lambda e: e.tensor_reduce(out=mwc[:, 0:1], in_=psS[:, 0:16], axis=AX.X, op=ALU.add), reads=("psSa", "mwc"), writes=("mwc",))
                O("dve", lambda e: e.tensor_reduce(out=mwc[:, 1:2], in_=psS[:, 16:48], axis=AX.X, op=ALU.add), reads=("psSa", "mwc"), writes=("mwc",))
                for j in range(2):
                    O("dve", lambda e, j=j: e.tensor_scalar(out=mwc[:, j:j + 1], in0=mwc[:, j:j + 1], scalar1=1.0 / (DH * DM), scalar2=1e-5, op0=ALU.mult, op1=ALU.max), reads=("mwc",), writes=("mwc",))
                    O("dve", lambda e, j=j: e.reciprocal(out=mwc[:, 2 + j:3 + j], in_=mwc[:, j:j + 1]), reads=("mwc",), writes=("mwc",))
                O("dve", lambda e: e.tensor_scalar(out=mwa[:, 0:1], in0=mwc[:, 0:1], scalar1=1.0 / 127.0, scalar2=None, op0=ALU.mult), reads=("mwc",), writes=("mwa",))
                O("dve", lambda e: e.tensor_scalar(out=mwa[:, 1:2], in0=mwc[:, 1:2], scalar1=0.5 / 127.0, scalar2=None, op0=ALU.mult), reads=("mwc", "mwa"), writes=("mwa",))

                for ci, (wn, c, hh, wid) in enumerate(pieces):
                    p = ci % 4
                    q = ci % 2
                    j = 2 if wn == "w1" else 3
                    O("sp", lambda e, wn=wn, c=c, hh=hh, wid=wid, p=p: e.dma_start(out=wstr[p][:, 0:wid], in_=src_ap(wn, c, hh, wid)), reads=(f"wstr{p}",), writes=(f"wstr{p}",), sem=f"d{p}", inc=16)
                    O("act", lambda e, p=p, q=q, wid=wid, j=j: e.activation(out=wtT[q][:, 0:wid], in_=wstr[p][:, 0:wid], func=AF.Copy, scale=mwc[:, j:j + 1]), reads=(f"wstr{p}", "mwc", f"wtT{q}"), writes=(f"wtT{q}",))
                    rne_round(wtT[q][:, 0:wid], (f"wtT{q}",), wtT[q][:, 0:wid], f"wtT{q}")
                    O("dve", lambda e, q=q, wid=wid: e.tensor_scalar(out=wqb[q][:, 0:wid], in0=wtT[q][:, 0:wid], scalar1=-1.0, scalar2=1.0, op0=ALU.max, op1=ALU.min), reads=(f"wtT{q}", f"wqb{q}"), writes=(f"wqb{q}",))
                    if wn == "w1":
                        O("sp", lambda e, c=c, hh=hh, q=q: e.dma_start(out=d_w1q[c * 128:(c + 1) * 128, hh * 2048:(hh + 1) * 2048], in_=wqb[q][:, 0:2048]), reads=(f"wqb{q}",), writes=("w1q_dram",), sem="ds", inc=16)
                    else:
                        O("sp", lambda e, c=c, q=q, wid=wid: e.dma_start(out=d_w2q[c * 128:(c + 1) * 128, :], in_=wqb[q][:, 0:wid]), reads=(f"wqb{q}",), writes=("w2q_dram",), sem="ds", inc=16)

        def emit_sub0(k1S, arow, src_dram, src_buf, copy_to=None, xs_ext=None, limit=""):
            with contextlib.ExitStack() as s0:
                zb = lambda n, sh, dt: s0.enter_context(nc.sbuf_tensor(_un(n), sh, dt))
                xs = xs_ext if xs_ext is not None else zb("xs0t", [128, 8 * T], F32)
                xbF = zb("xbF", [128, 8 * 512], BF16)
                xqF = zb("xqF", [128, 8 * 512], BF16)
                runmx = zb("runmx", [128, 512], BF16)
                runmn = zb("runmn", [128, 512], BF16)
                cst = zb("cst", [128, 16], F32)
                asb = zb("asb", [128, 512], F32)
                bbt = zb("bbt", [128, 512], F32)
                xa = [zb("xa0", [128, 512], F32), zb("xa1", [128, 512], F32)]
                tts = [zb("tts0", [128, 512], F32), zb("tts1", [128, 512], F32)]

                if src_dram is not None:
                    rd = (src_buf,) if src_buf else ()
                    for fc in range(8):
                        O("sp", lambda e, fc=fc: e.dma_start(out=xs[:, fc * T:(fc + 1) * T], in_=src_dram[fc * 128:(fc + 1) * 128, :]), reads=rd, writes=(f"xs{fc}",), sem=f"d{fc % 4}", inc=16)
                    em.join(tuple(f"xs{fc}" for fc in range(8)))
                if copy_to is not None:
                    for fc in range(8):
                        O("sp", lambda e, fc=fc: e.dma_start(out=copy_to[fc * 128:(fc + 1) * 128, :], in_=xs[:, fc * T:(fc + 1) * T]), reads=(f"xs{fc}",), writes=("xwork",), sem="ds", inc=16)

                if limit == "loads":
                    return
                _ntg = 4
                if limit.startswith("n"):
                    _ntg = int(limit[1])
                for tg in range(_ntg):
                    t0 = tg * 512
                    for fc in range(8):
                        xsl = xs[:, fc * T + t0: fc * T + t0 + 512]
                        xbl = xbF[:, fc * 512:(fc + 1) * 512]
                        O("act", lambda e, xsl=xsl, xbl=xbl: e.activation(out=xbl, in_=xsl, func=AF.Copy), reads=(f"xs{fc}", f"xb{fc}"), writes=(f"xb{fc}",))
                        O("act", lambda e, fc=fc, xbl=xbl: e.activation(out=xqF[:, fc * 512:(fc + 1) * 512], in_=xbl, func=AF.Square), reads=(f"xb{fc}", f"xq{fc}"), writes=(f"xq{fc}",))
                        if fc == 0:
                            O("dve", lambda e, xbl=xbl: e.tensor_copy(out=runmx[:], in_=xbl), reads=(f"xb{fc}", "runmx"), writes=("runmx",))
                            O("dve", lambda e, xbl=xbl: e.tensor_copy(out=runmn[:], in_=xbl), reads=(f"xb{fc}", "runmn"), writes=("runmn",))
                        else:
                            O("dve", lambda e, xbl=xbl: e.tensor_tensor(out=runmx[:], in0=runmx[:], in1=xbl, op=ALU.max), reads=(f"xb{fc}", "runmx"), writes=("runmx",))
                            O("dve", lambda e, xbl=xbl: e.tensor_tensor(out=runmn[:], in0=runmn[:], in1=xbl, op=ALU.min), reads=(f"xb{fc}", "runmn"), writes=("runmn",))
                    for ti in range(4):
                        for fc in range(8):
                            O("pe", lambda e, ti=ti, fc=fc: e.matmul(psS[:, ti:ti + 1], xbF[:, fc * 512 + ti * 128: fc * 512 + (ti + 1) * 128], onescol_bf[:], start=(fc == 0), stop=(fc == 7)), reads=(f"xb{fc}", "onescol", "psSa"), writes=("psSa",), inc=(1 if fc == 7 else 0))
                    for ti in range(4):
                        for fc in range(8):
                            O("pe", lambda e, ti=ti, fc=fc: e.matmul(psS[:, 4 + ti:5 + ti], xqF[:, fc * 512 + ti * 128: fc * 512 + (ti + 1) * 128], onescol_bf[:], start=(fc == 0), stop=(fc == 7)), reads=(f"xq{fc}", "onescol", "psSa"), writes=("psSa",), inc=(1 if fc == 7 else 0))
                    cmean = cst[:, 0:4]
                    cvar = cst[:, 4:8]
                    ca = cst[:, 12:16]
                    O("dve", lambda e, cmean=cmean: e.tensor_scalar(out=cmean, in0=psS[:, 0:4], scalar1=1.0 / DM, scalar2=None, op0=ALU.mult), reads=("psSa", "cst"), writes=("cst",))
                    O("dve", lambda e, cmean=cmean, cvar=cvar: e.tensor_mul(out=cvar, in0=cmean, in1=cmean), reads=("cst",), writes=("cst",))
                    O("dve", lambda e, cvar=cvar: e.scalar_tensor_tensor(out=cvar, in0=psS[:, 4:8], scalar=1.0 / DM, in1=cvar, op0=ALU.mult, op1=ALU.subtract), reads=("psSa", "cst"), writes=("cst",))
                    O("act", lambda e, cvar=cvar: e.activation(out=cvar, in_=cvar, func=AF.Sqrt, bias=eps5[:]), reads=("cst", "eps5"), writes=("cst",))
                    O("dve", lambda e, cvar=cvar: e.reciprocal(out=cvar, in_=cvar), reads=("cst",), writes=("cst",))
                    for ti in range(4):
                        O("pe", lambda e, ti=ti: e.transpose(pst[:], runmx[:, ti * 128:(ti + 1) * 128], identb[:]), reads=("runmx", "identb", "pst"), writes=("pst",))
                        O("dve", lambda e, ti=ti: e.tensor_reduce(out=cst[:, 12 + ti:13 + ti], in_=pst[:], axis=AX.X, op=ALU.max), reads=("pst", "cst"), writes=("cst",))
                    for ti in range(4):
                        O("pe", lambda e, ti=ti: e.transpose(pst[:], runmn[:, ti * 128:(ti + 1) * 128], identb[:]), reads=("runmn", "identb", "pst"), writes=("pst",))
                        O("dve", lambda e, ti=ti: e.tensor_reduce(out=cst[:, 8 + ti:9 + ti], in_=pst[:], axis=AX.X, op=ALU.min), reads=("pst", "cst"), writes=("cst",))
                    O("dve", lambda e, ca=ca, cmean=cmean: e.tensor_sub(out=ca, in0=ca, in1=cmean), reads=("cst",), writes=("cst",))
                    O("dve", lambda e, cmean=cmean: e.tensor_sub(out=cst[:, 8:12], in0=cmean, in1=cst[:, 8:12]), reads=("cst",), writes=("cst",))
                    O("dve", lambda e, ca=ca: e.tensor_tensor(out=ca, in0=ca, in1=cst[:, 8:12], op=ALU.max), reads=("cst",), writes=("cst",))
                    O("dve", lambda e, ca=ca, cvar=cvar: e.tensor_tensor(out=ca, in0=ca, in1=cvar, op=ALU.mult), reads=("cst",), writes=("cst",))
                    O("dve", lambda e, ca=ca: e.tensor_scalar(out=ca, in0=ca, scalar1=1e-5, scalar2=None, op0=ALU.max), reads=("cst",), writes=("cst",))
                    O("dve", lambda e: e.tensor_scalar(out=cst[:, 8:12], in0=cst[:, 12:16], scalar1=mwa[:, 0:1], scalar2=None, op0=ALU.mult), reads=("cst", "mwa"), writes=("cst",))
                    for ti in range(4):
                        O("pe", lambda e, ti=ti: e.matmul(psF[0:1, ti * 128:(ti + 1) * 128], cst[:, 8 + ti:9 + ti], identf[:], start=True, stop=True), reads=("cst", "identf", "psF"), writes=("psF",))
                    O("dve", lambda e, t0=t0: e.tensor_copy(out=arow[0:1, t0:t0 + 512], in_=psF[0:1, 0:512]), reads=("psF", "arow"), writes=("arow",))
                    O("dve", lambda e: e.reciprocal(out=cst[:, 8:12], in_=cst[:, 12:16]), reads=("cst",), writes=("cst",))
                    O("dve", lambda e: e.tensor_scalar(out=cst[:, 8:12], in0=cst[:, 8:12], scalar1=127.0, scalar2=None, op0=ALU.mult), reads=("cst",), writes=("cst",))
                    O("dve", lambda e, cvar=cvar: e.tensor_tensor(out=cst[:, 8:12], in0=cst[:, 8:12], in1=cvar, op=ALU.mult), reads=("cst",), writes=("cst",))
                    O("dve", lambda e, cmean=cmean: e.tensor_tensor(out=cst[:, 12:16], in0=cmean, in1=cst[:, 8:12], op=ALU.mult), reads=("cst",), writes=("cst",))
                    for ti in range(4):
                        O("pe", lambda e, ti=ti: e.matmul(psE[0:1, ti * 128:(ti + 1) * 128], cst[:, 8 + ti:9 + ti], identf[:], start=True, stop=True), reads=("cst", "identf", "psE"), writes=("psE",))
                    O("dve", lambda e: e.tensor_copy(out=rowC[0:1, 0:512], in_=psE[0:1, 0:512]), reads=("psE", "rowCs"), writes=("rowCs",))
                    for ti in range(4):
                        O("pe", lambda e, ti=ti: e.matmul(psF[0:1, ti * 128:(ti + 1) * 128], cst[:, 12 + ti:13 + ti], identf[:], start=True, stop=True), reads=("cst", "identf", "psF"), writes=("psF",))
                    O("dve", lambda e: e.tensor_copy(out=rowD[0:1, 0:512], in_=psF[0:1, 0:512]), reads=("psF", "rowDs"), writes=("rowDs",))
                    O("pe", lambda e: e.matmul(psE[:, 0:512], onesrow[:], rowC[0:1, 0:512], start=True, stop=True), reads=("rowCs", "onesrow", "psE"), writes=("psE",))
                    O("dve", lambda e: e.tensor_copy(out=asb[:], in_=psE[:, 0:512]), reads=("psE", "asb"), writes=("asb",))
                    O("pe", lambda e: e.matmul(psF[:, 0:512], onesrow[:], rowD[0:1, 0:512], start=True, stop=True), reads=("rowDs", "onesrow", "psF"), writes=("psF",))
                    O("dve", lambda e: e.tensor_copy(out=bbt[:], in_=psF[:, 0:512]), reads=("psF", "bbt"), writes=("bbt",))
                    for fc in range(8):
                        xsl = xs[:, fc * T + t0: fc * T + t0 + 512]
                        p2_ = fc % 2
                        O("pool", lambda e, xsl=xsl, p2_=p2_: e.tensor_mul(out=xa[p2_][:], in0=xsl, in1=asb[:]), reads=(f"xs{fc}", "asb", f"xa{p2_}"), writes=(f"xa{p2_}",))
                        O("dve", lambda e, p2_=p2_: e.tensor_sub(out=tts[p2_][:], in0=xa[p2_][:], in1=bbt[:]), reads=(f"xa{p2_}", "bbt", f"tts{p2_}"), writes=(f"tts{p2_}",))
                        rne_round(tts[p2_][:], (f"tts{p2_}",), k1S[:, fc * T + t0: fc * T + t0 + 512], f"k1_{fc}")

        def emit_sub12(k1S, arow, dst_dram, dst_buf, w1q_ext=None):
            with contextlib.ExitStack() as s1:
                zb = lambda n, sh, dt: s1.enter_context(nc.sbuf_tensor(_un(n), sh, dt))
                w1qbf = w1q_ext if w1q_ext is not None else zb("w1qbf", [128, 8 * DH], BF16)
                diag3 = zb("diag3", [128, 32 * 384], BF16)
                uS = zb("uS", [128, 32 * GRP], F16)
                k2S = zb("k2S", [128, 32 * GRP], BF16)
                hbf = [zb("hbf0", [128, 260], BF16), zb("hbf1", [128, 260], BF16)]
                ab = zb("ab", [128, 260], F32)
                s2b2 = zb("s2b2", [128, 512], F32)
                betb = zb("betb", [128, 256], F32)
                ufmx = zb("ufmx", [128, 512], F16)
                ufmn = zb("ufmn", [128, 512], F16)
                urun = zb("urun", [128, 512], F32)
                colsc2 = zb("colsc2", [128, 4], F32)
                t32 = [zb("t32a", [128, 512], F32), zb("t32b", [128, 512], F32)]
                tmpe = [zb("tmpe0", [128, 256], F32), zb("tmpe1", [128, 256], F32)]
                w2t_ = [zb("w2ta", [128, DM], BF16), zb("w2tb", [128, DM], BF16)]

                if w1q_ext is None:
                    for fc in range(8):
                        O("sp", lambda e, fc=fc: e.dma_start(out=w1qbf[:, fc * DH:(fc + 1) * DH], in_=d_w1q[fc * 128:(fc + 1) * 128, :]), reads=("w1q_dram",), writes=(f"w1qbf{fc}",), sem=f"d{fc % 2}", inc=16)
                    em.join(tuple(f"w1qbf{fc}" for fc in range(8)))
                for dc in range(32):
                    for k in range(3):
                        O("dve", lambda e, dc=dc, k=k: e.tensor_scalar(out=diag3[:, dc * 384 + k * 128: dc * 384 + (k + 1) * 128], in0=identf[:], scalar1=cwS[:, dc * 3 + k: dc * 3 + k + 1], scalar2=None, op0=ALU.mult), reads=("identf", "cwS", f"diag{dc}"), writes=(f"diag{dc}",))

                acc_ps = [psB, psB, psC, psC, psD, psD, psE, psE]
                acc_off = [0, 256, 0, 256, 0, 256, 0, 256]
                acc_nm = ["psB", "psB", "psC", "psC", "psD", "psD", "psE", "psE"]

                for g in range(NG):
                    g0 = g * GRP
                    lo = max(g0 - 1, 0)
                    hi = min(g0 + GRP + 1, T)
                    n = hi - lo
                    goff = g0 - lo
                    O("pe", lambda e, lo=lo, n=n: e.matmul(psA[:, 0:n], onesrow[:], arow[0:1, lo:lo + n], start=True, stop=True), reads=("arow", "onesrow", "psA"), writes=("psA",))
                    O("dve", lambda e, n=n: e.tensor_copy(out=ab[:, 0:n], in_=psA[:, 0:n]), reads=("psA", "ab"), writes=("ab",))
                    for dc in range(32):
                        hp, hpb = (psE, "psE") if dc % 2 == 0 else (psF, "psF")
                        cp, cpb = (psC, "psC") if dc % 2 == 0 else (psD, "psD")
                        for fc in range(8):
                            O("pe", lambda e, fc=fc, dc=dc, hp=hp, lo=lo, hi=hi, n=n: e.matmul(hp[:, 0:n], w1qbf[:, fc * DH + dc * 128: fc * DH + (dc + 1) * 128], k1S[:, fc * T + lo: fc * T + hi], start=(fc == 0), stop=(fc == 7)), reads=(f"w1qbf{fc}", f"k1_{fc}", hpb), writes=(hpb,), inc=(1 if fc == 7 else 0))
                        p = dc % 2
                        O("dve", lambda e, hp=hp, p=p, n=n: e.tensor_tensor(out=hbf[p][:, 0:n], in0=hp[:, 0:n], in1=ab[:, 0:n], op=ALU.mult), reads=(hpb, "ab", f"hbf{p}"), writes=(f"hbf{p}",))
                        taps = []
                        for k in (1, 0, 2):
                            j0 = goff + (k - 1)
                            lo_in = max(j0, 0)
                            hi_in = min(j0 + GRP, n)
                            o0 = lo_in - j0
                            o1 = o0 + (hi_in - lo_in)
                            taps.append((k, lo_in, hi_in, o0, o1))
                        for idx, (k, lo_in, hi_in, o0, o1) in enumerate(taps):
                            O("pe", lambda e, dc=dc, k=k, cp=cp, p=p, lo_in=lo_in, hi_in=hi_in, o0=o0, o1=o1, idx=idx: e.matmul(cp[:, o0:o1], diag3[:, dc * 384 + k * 128: dc * 384 + (k + 1) * 128], hbf[p][:, lo_in:hi_in], start=(idx == 0), stop=(idx == 2)), reads=(f"diag{dc}", f"hbf{p}", cpb), writes=(cpb,), inc=(1 if idx == 2 else 0))
                        O("act", lambda e, cp=cp, dc=dc: e.activation(out=uS[:, dc * GRP:(dc + 1) * GRP], in_=cp[:, 0:GRP], func=AF.Silu, bias=cbS[:, dc:dc + 1]), reads=(cpb, "cbS", f"uS{dc}"), writes=(f"uS{dc}",))
                        if dc == 1:
                            O("dve", lambda e: e.tensor_copy(out=ufmx[:], in_=uS[:, 0:2 * GRP]), reads=("uS0", "uS1", "ufmx"), writes=("ufmx",))
                            O("dve", lambda e: e.tensor_copy(out=ufmn[:], in_=uS[:, 0:2 * GRP]), reads=("uS0", "uS1", "ufmn"), writes=("ufmn",))
                        elif dc % 2 == 1:
                            O("dve", lambda e, dc=dc: e.tensor_tensor(out=ufmx[:], in0=ufmx[:], in1=uS[:, (dc - 1) * GRP:(dc + 1) * GRP], op=ALU.max), reads=(f"uS{dc - 1}", f"uS{dc}", "ufmx"), writes=("ufmx",))
                            O("dve", lambda e, dc=dc: e.tensor_tensor(out=ufmn[:], in0=ufmn[:], in1=uS[:, (dc - 1) * GRP:(dc + 1) * GRP], op=ALU.min), reads=(f"uS{dc - 1}", f"uS{dc}", "ufmn"), writes=("ufmn",))
                    O("dve", lambda e: e.tensor_tensor(out=urun[:, 0:256], in0=ufmx[:, 0:256], in1=ufmx[:, 256:512], op=ALU.max), reads=("ufmx", "urun"), writes=("urun",))
                    O("dve", lambda e: e.tensor_tensor(out=urun[:, 256:512], in0=ufmn[:, 0:256], in1=ufmn[:, 256:512], op=ALU.min), reads=("ufmn", "urun"), writes=("urun",))
                    for ti in range(2):
                        O("pe", lambda e, ti=ti: e.transpose(psS[:, 0:128], urun[:, ti * 128:(ti + 1) * 128], identf[:]), reads=("urun", "identf", "psSa"), writes=("psSa",))
                        O("dve", lambda e, ti=ti: e.tensor_reduce(out=colsc2[:, ti:ti + 1], in_=psS[:, 0:128], axis=AX.X, op=ALU.max), reads=("psSa", "colsc2"), writes=("colsc2",))
                    for ti in range(2):
                        O("pe", lambda e, ti=ti: e.transpose(psS[:, 0:128], urun[:, 256 + ti * 128: 256 + (ti + 1) * 128], identf[:]), reads=("urun", "identf", "psSa"), writes=("psSa",))
                        O("dve", lambda e, ti=ti: e.tensor_reduce(out=colsc2[:, 2 + ti:3 + ti], in_=psS[:, 0:128], axis=AX.X, op=ALU.min), reads=("psSa", "colsc2"), writes=("colsc2",))
                    O("dve", lambda e: e.tensor_scalar(out=colsc2[:, 2:4], in0=colsc2[:, 2:4], scalar1=-1.0, scalar2=None, op0=ALU.mult), reads=("colsc2",), writes=("colsc2",))
                    O("dve", lambda e: e.tensor_tensor(out=colsc2[:, 0:2], in0=colsc2[:, 0:2], in1=colsc2[:, 2:4], op=ALU.max), reads=("colsc2",), writes=("colsc2",))
                    O("dve", lambda e: e.tensor_scalar(out=colsc2[:, 0:2], in0=colsc2[:, 0:2], scalar1=1e-5, scalar2=None, op0=ALU.max), reads=("colsc2",), writes=("colsc2",))
                    for ti in range(2):
                        O("pe", lambda e, ti=ti: e.matmul(psF[0:1, ti * 128:(ti + 1) * 128], colsc2[:, ti:ti + 1], identf[:], start=True, stop=True), reads=("colsc2", "identf", "psF"), writes=("psF",))
                    O("dve", lambda e: e.tensor_copy(out=rowE[0:1, 0:256], in_=psF[0:1, 0:256]), reads=("psF", "rowE"), writes=("rowE",))
                    O("pe", lambda e: e.matmul(psA[:, 0:256], onesrow[:], rowE[0:1, 0:256], start=True, stop=True), reads=("rowE", "onesrow", "psA"), writes=("psA",))
                    O("dve", lambda e: e.tensor_scalar(out=betb[:], in0=psA[:, 0:256], scalar1=mwa[:, 1:2], scalar2=None, op0=ALU.mult), reads=("psA", "mwa", "betb"), writes=("betb",))
                    O("dve", lambda e: e.reciprocal(out=colsc2[:, 2:4], in_=colsc2[:, 0:2]), reads=("colsc2",), writes=("colsc2",))
                    for ti in range(2):
                        O("pe", lambda e, ti=ti: e.matmul(psF[0:1, ti * 128:(ti + 1) * 128], colsc2[:, 2 + ti:3 + ti], identf[:], start=True, stop=True), reads=("colsc2", "identf", "psF"), writes=("psF",))
                    O("dve", lambda e: e.tensor_copy(out=rowC[0:1, 0:256], in_=psF[0:1, 0:256]), reads=("psF", "rowCs"), writes=("rowCs",))
                    O("dve", lambda e: e.tensor_copy(out=rowC[0:1, 256:512], in_=psF[0:1, 0:256]), reads=("psF", "rowCs"), writes=("rowCs",))
                    O("pe", lambda e: e.matmul(psA[:, 0:512], c127row[:], rowC[0:1, 0:512], start=True, stop=True), reads=("rowCs", "c127row", "psA"), writes=("psA",))
                    O("dve", lambda e: e.tensor_copy(out=s2b2[:], in_=psA[:, 0:512]), reads=("psA", "s2b2"), writes=("s2b2",))
                    for jj in range(16):
                        p2_ = jj % 2
                        O("pool", lambda e, jj=jj, p2_=p2_: e.tensor_tensor(out=t32[p2_][:], in0=uS[:, jj * 512:(jj + 1) * 512], in1=s2b2[:], op=ALU.mult), reads=(f"uS{2 * jj}", f"uS{2 * jj + 1}", "s2b2", f"t32_{p2_}"), writes=(f"t32_{p2_}",))
                        rne_round(t32[p2_][:], (f"t32_{p2_}",), k2S[:, jj * 512:(jj + 1) * 512], f"k2j{jj}")
                    for fc2 in range(32):
                        p = fc2 % 2
                        O("sp", lambda e, fc2=fc2, p=p: e.dma_start(out=w2t_[p][:], in_=d_w2q[fc2 * 128:(fc2 + 1) * 128, :]), reads=("w2q_dram", f"w2t{p}"), writes=(f"w2t{p}",), sem=f"d{2 + p}", inc=16)
                        for mc in range(8):
                            O("pe", lambda e, p=p, mc=mc, fc2=fc2: e.matmul(acc_ps[mc][:, acc_off[mc]:acc_off[mc] + 256], w2t_[p][:, mc * 128:(mc + 1) * 128], k2S[:, fc2 * 256:(fc2 + 1) * 256], start=(fc2 == 0 and mc % 2 == 0), stop=(fc2 == 31 and mc % 2 == 1)), reads=(f"w2t{p}", f"k2j{fc2 // 2}", acc_nm[mc]), writes=(acc_nm[mc],), inc=(1 if mc == 7 or fc2 == 31 else 0))
                    for mc in range(8):
                        p = mc % 2
                        O("dve", lambda e, mc=mc, p=p: e.tensor_tensor(out=tmpe[p][:], in0=acc_ps[mc][:, acc_off[mc]:acc_off[mc] + 256], in1=betb[:], op=ALU.mult), reads=(acc_nm[mc], "betb", f"tmpe{p}"), writes=(f"tmpe{p}",))
                        O("pool", lambda e, mc=mc, g0=g0, p=p: e.dma_start(out=dst_dram[mc * 128:(mc + 1) * 128, g0:g0 + GRP], in_=tmpe[p][:], accum_op=ALU.add), reads=(f"tmpe{p}", dst_buf), writes=(dst_buf,), sem="wr", inc=16)

        def emit_attn(xs):
            with contextlib.ExitStack() as at:
                zb = lambda n, sh, dt: at.enter_context(nc.sbuf_tensor(_un(n), sh, dt))
                yn = zb("yn", [128, 8 * T], BF16)
                oall = zb("oall", [128, 8 * T], BF16)
                cs2 = zb("cs2", [128, T], BF16)
                sc2 = zb("sc2", [128, T], BF16)
                qraw = zb("qraw", [128, T], BF16)
                qnbf = zb("qnbf", [128, T], BF16)
                qkrot = zb("qkrot", [128, 2 * T], BF16)
                pa = zb("pa", [128, T], BF16)
                pb = zb("pb", [128, T], BF16)
                vtm = zb("vtm", [128, T], BF16)
                wabf = [zb("wabf0", [128, DM], BF16), zb("wabf1", [128, DM], BF16)]
                pblk = [zb("pblk0", [128, 512], BF16), zb("pblk1", [128, 512], BF16)]
                nrow = zb("nrow", [1, 512], F32)
                nb = zb("nb", [128, 512], F32)
                tsc0 = zb("tsc0", [128, 512], F32)
                tsc1 = zb("tsc1", [128, 512], F32)
                mb2 = zb("mb2", [128, 512], F32)
                rstd2 = zb("rstd2", [128, 512], F32)
                xbd = [zb("xbda", [128, 512], BF16), zb("xbdb", [128, 512], BF16)]
                xqd = [zb("xqda", [128, 512], BF16), zb("xqdb", [128, 512], BF16)]
                sqb = zb("sqb", [128, 512], BF16)
                rqs = zb("rqs", [128, 16], F32)
                rks = zb("rks", [128, 16], F32)

                for fc in range(8):
                    O("sp", lambda e, fc=fc: e.dma_start(out=xs[:, fc * T:(fc + 1) * T], in_=d_xwork[fc * 128:(fc + 1) * 128, :]), reads=("xwork",), writes=(f"xs{fc}",), sem=f"d{fc % 2}", inc=16)
                em.join(tuple(f"xs{fc}" for fc in range(8)))

                for tab, ph, phL_, phb, tb in ((cs2, phc, phcL, "phc", "cs2"), (sc2, phs, phsL, "phs", "sc2")):
                    for sl in range(4):
                        O("pool", lambda e, sl=sl: e.iota(out=rowE[:], pattern=[[1, 512]], base=sl * 512, channel_multiplier=0, allow_small_or_imprecise_dtypes=True), reads=("rowE",), writes=("rowE",))
                        O("pe", lambda e: e.matmul(psA[:, 0:512], freq2[:], rowE[:], start=True, stop=True), reads=("freq2", "rowE", "psA"), writes=("psA",))
                        O("dve", lambda e, ph=ph, phL_=phL_, phb=phb: e.tensor_scalar(out=tsc0[:], in0=psA[:, 0:512], scalar1=ph[:], scalar2=phL_[:], op0=ALU.add, op1=ALU.add), reads=("psA", phb, phb + "L", "tsc0"), writes=("tsc0",))
                        O("dve", lambda e: e.tensor_scalar(out=tsc1[:], in0=tsc0[:], scalar1=1.0 / TWO_PI, scalar2=None, op0=ALU.mult), reads=("tsc0", "tsc1"), writes=("tsc1",))
                        O("dve", lambda e: e.tensor_scalar(out=tsc1[:], in0=tsc1[:], scalar1=mcol[:], scalar2=nmcol[:], op0=ALU.add, op1=ALU.add), reads=("tsc1", "mcol", "nmcol"), writes=("tsc1",))
                        O("dve", lambda e: e.scalar_tensor_tensor(out=tsc0[:], in0=tsc1[:], scalar=TWO_PI, in1=tsc0[:], op0=ALU.mult, op1=ALU.subtract), reads=("tsc1", "tsc0"), writes=("tsc0",))
                        O("dve", lambda e: e.tensor_scalar(out=tsc0[:], in0=tsc0[:], scalar1=-CLAMP, scalar2=CLAMP, op0=ALU.max, op1=ALU.min), reads=("tsc0",), writes=("tsc0",))
                        O("act", lambda e, tab=tab, sl=sl, tb=tb: e.activation(out=tab[:, sl * 512:(sl + 1) * 512], in_=tsc0[:], func=AF.Sin, scale=m1col[:]), reads=("tsc0", "m1col", tb), writes=(tb,))

                for tg in range(4):
                    t0 = tg * 512
                    for fc in range(8):
                        xsl = xs[:, fc * T + t0: fc * T + t0 + 512]
                        p_ = fc % 2
                        xbl = xbd[p_]
                        xql = xqd[p_]
                        O("act", lambda e, xsl=xsl, xbl=xbl, p_=p_: e.activation(out=xbl[:], in_=xsl, func=AF.Copy), reads=(f"xs{fc}", f"xbd{p_}"), writes=(f"xbd{p_}",))
                        O("act", lambda e, xbl=xbl, xql=xql, p_=p_: e.activation(out=xql[:], in_=xbl[:], func=AF.Square), reads=(f"xbd{p_}", f"xqd{p_}"), writes=(f"xqd{p_}",))
                        O("pe", lambda e, fc=fc, xbl=xbl, p_=p_: e.matmul(psE[:, 0:512], ones_bf[:], xbl[:], start=(fc == 0), stop=(fc == 7)), reads=(f"xbd{p_}", "ones_bf", "psE"), writes=("psE",))
                        O("pe", lambda e, fc=fc, xql=xql, p_=p_: e.matmul(psF[:, 0:512], ones_bf[:], xql[:], start=(fc == 0), stop=(fc == 7)), reads=(f"xqd{p_}", "ones_bf", "psF"), writes=("psF",))
                    O("dve", lambda e: e.tensor_scalar(out=mb2[:], in0=psE[:, 0:512], scalar1=1.0 / DM, scalar2=None, op0=ALU.mult), reads=("psE", "mb2"), writes=("mb2",))
                    O("dve", lambda e: e.tensor_mul(out=tsc0[:], in0=mb2[:], in1=mb2[:]), reads=("mb2", "tsc0"), writes=("tsc0",))
                    O("dve", lambda e: e.scalar_tensor_tensor(out=tsc0[:], in0=psF[:, 0:512], scalar=1.0 / DM, in1=tsc0[:], op0=ALU.mult, op1=ALU.subtract), reads=("psF", "tsc0"), writes=("tsc0",))
                    O("act", lambda e: e.activation(out=tsc0[:], in_=tsc0[:], func=AF.Sqrt, bias=eps5[:]), reads=("tsc0", "eps5"), writes=("tsc0",))
                    O("dve", lambda e: e.reciprocal(out=rstd2[:], in_=tsc0[:]), reads=("tsc0", "rstd2"), writes=("rstd2",))
                    for fc in range(8):
                        xsl = xs[:, fc * T + t0: fc * T + t0 + 512]
                        O("dve", lambda e, xsl=xsl: e.tensor_sub(out=tsc1[:], in0=xsl, in1=mb2[:]), reads=(f"xs{fc}", "mb2", "tsc1"), writes=("tsc1",))
                        O("dve", lambda e, fc=fc, t0=t0: e.tensor_tensor(out=yn[:, fc * T + t0: fc * T + t0 + 512], in0=tsc1[:], in1=rstd2[:], op=ALU.mult), reads=("tsc1", "rstd2", "yn"), writes=("yn",))

                def proj(dsrc, dbuf, hidx, psum, psb, dst_ap, dst_buf, slot):
                    O("pool", lambda e, hidx=hidx, slot=slot, dsrc=dsrc: e.dma_start(out=wabf[slot][:].rearrange("p (c d) -> p c d", c=8), in_=dsrc[:, hidx * 128:(hidx + 1) * 128].rearrange("(c p) d -> p c d", p=128)), reads=(f"wabf{slot}",), writes=(f"wabf{slot}",), sem=("wa" if slot == 0 else "wb"), inc=16)
                    for tg in range(4):
                        for fc in range(8):
                            O("pe", lambda e, fc=fc, tg=tg, slot=slot, psum=psum: e.matmul(psum[:, 0:512], wabf[slot][:, fc * 128:(fc + 1) * 128], yn[:, fc * T + tg * 512: fc * T + (tg + 1) * 512], start=(fc == 0), stop=(fc == 7)), reads=(f"wabf{slot}", "yn", psb), writes=(psb,), inc=(1 if fc == 7 else 0))
                        O("dve", lambda e, tg=tg, psum=psum: e.tensor_copy(out=dst_ap[:, tg * 512:(tg + 1) * 512], in_=psum[:, 0:512]), reads=(psb, dst_buf), writes=(dst_buf,))

                for h in range(H):
                    proj(d_wq, "wq", h, psA, "psA", qraw, "qraw", h % 2)
                    for tg in range(4):
                        O("act", lambda e, tg=tg: e.activation(out=sqb[:], in_=qraw[:, tg * 512:(tg + 1) * 512], func=AF.Square), reads=("qraw", "sqb"), writes=("sqb",))
                        for ti in range(4):
                            O("pe", lambda e, tg=tg, ti=ti: e.matmul(psS[:, 280 + tg * 4 + ti: 281 + tg * 4 + ti], sqb[:, ti * 128:(ti + 1) * 128], onescol_bf[:], start=True, stop=True), reads=("sqb", "onescol", "psSq"), writes=("psSq",))
                    O("act", lambda e: e.activation(out=rqs[:], in_=psS[:, 280:296], func=AF.Sqrt, scale=c128[:], bias=eps6[:]), reads=("psSq", "c128", "eps6", "rqs"), writes=("rqs",))
                    O("dve", lambda e: e.reciprocal(out=rqs[:], in_=rqs[:]), reads=("rqs",), writes=("rqs",))
                    for tg in range(4):
                        for ti in range(4):
                            O("pe", lambda e, tg=tg, ti=ti: e.matmul(psF[0:1, ti * 128:(ti + 1) * 128], rqs[:, tg * 4 + ti: tg * 4 + ti + 1], identf[:], start=True, stop=True), reads=("rqs", "identf", "psF"), writes=("psF",))
                        O("dve", lambda e: e.tensor_copy(out=rowE[0:1, 0:512], in_=psF[0:1, 0:512]), reads=("psF", "rowE"), writes=("rowE",))
                        O("pe", lambda e: e.matmul(psB[:, 0:512], onesrow[:], rowE[0:1, 0:512], start=True, stop=True), reads=("rowE", "onesrow", "psB"), writes=("psB",))
                        O("dve", lambda e, tg=tg: e.tensor_tensor(out=qnbf[:, tg * 512:(tg + 1) * 512], in0=qraw[:, tg * 512:(tg + 1) * 512], in1=psB[:, 0:512], op=ALU.mult), reads=("qraw", "psB", "qnbf"), writes=("qnbf",))
                    O("dve", lambda e: e.tensor_tensor(out=pa[:], in0=qnbf[:], in1=cs2[:], op=ALU.mult), reads=("qnbf", "cs2", "pa"), writes=("pa",))
                    O("pool", lambda e: e.tensor_tensor(out=pb[:], in0=qnbf[:], in1=sc2[:], op=ALU.mult), reads=("qnbf", "sc2", "pb"), writes=("pb",))
                    for sl4 in range(4):
                        O("pe", lambda e, sl4=sl4: e.matmul(psC[:, 0:512], selE[:], pa[:, sl4 * 512:(sl4 + 1) * 512], start=True, stop=False), reads=("selE", "pa", "psC"), writes=("psC",), inc=0)
                        O("pe", lambda e, sl4=sl4: e.matmul(psC[:, 0:512], selO[:], pb[:, sl4 * 512:(sl4 + 1) * 512], start=False, stop=True), reads=("selO", "pb", "psC"), writes=("psC",))
                        O("dve", lambda e, sl4=sl4: e.tensor_copy(out=qkrot[:, sl4 * 512:(sl4 + 1) * 512], in_=psC[:, 0:512]), reads=("psC", "qkrot"), writes=("qkrot",))

                    proj(d_wk, "wk", h, psB, "psB", qraw, "qraw", (h + 1) % 2)
                    for tg in range(4):
                        O("act", lambda e, tg=tg: e.activation(out=sqb[:], in_=qraw[:, tg * 512:(tg + 1) * 512], func=AF.Square), reads=("qraw", "sqb"), writes=("sqb",))
                        for ti in range(4):
                            O("pe", lambda e, tg=tg, ti=ti: e.matmul(psS[:, 256 + tg * 4 + ti: 257 + tg * 4 + ti], sqb[:, ti * 128:(ti + 1) * 128], onescol_bf[:], start=True, stop=True), reads=("sqb", "onescol", "psSc"), writes=("psSc",))
                    O("act", lambda e: e.activation(out=rks[:], in_=psS[:, 256:272], func=AF.Sqrt, scale=c128[:], bias=eps6[:]), reads=("psSc", "c128", "eps6", "rks"), writes=("rks",))
                    O("dve", lambda e: e.reciprocal(out=rks[:], in_=rks[:]), reads=("rks",), writes=("rks",))
                    O("dve", lambda e: e.tensor_scalar(out=rks[:], in0=rks[:], scalar1=1.0 / 128.0, scalar2=None, op0=ALU.mult), reads=("rks",), writes=("rks",))
                    O("dve", lambda e: e.tensor_tensor(out=pa[:], in0=qraw[:], in1=cs2[:], op=ALU.mult), reads=("qraw", "cs2", "pa"), writes=("pa",))
                    O("pool", lambda e: e.tensor_tensor(out=pb[:], in0=qraw[:], in1=sc2[:], op=ALU.mult), reads=("qraw", "sc2", "pb"), writes=("pb",))
                    for sl4 in range(4):
                        O("pe", lambda e, sl4=sl4: e.matmul(psC[:, 0:512], selE[:], pa[:, sl4 * 512:(sl4 + 1) * 512], start=True, stop=False), reads=("selE", "pa", "psC"), writes=("psC",), inc=0)
                        O("pe", lambda e, sl4=sl4: e.matmul(psC[:, 0:512], selO[:], pb[:, sl4 * 512:(sl4 + 1) * 512], start=False, stop=True), reads=("selO", "pb", "psC"), writes=("psC",))
                        O("dve", lambda e, sl4=sl4: e.tensor_copy(out=qkrot[:, T + sl4 * 512: T + (sl4 + 1) * 512], in_=psC[:, 0:512]), reads=("psC", "qkrot"), writes=("qkrot",))

                    proj(d_wv, "wv", h, psA, "psA", qnbf, "qnbf", h % 2)
                    for tc in range(16):
                        O("sp", lambda e, tc=tc: e.dma_start_transpose(out=vtm[:, tc * 128:(tc + 1) * 128], in_=qnbf[:, tc * 128:(tc + 1) * 128]), reads=("qnbf", "vtm"), writes=("vtm",), sem="ds", inc=16)
                    em.join(("vtm",))

                    for ig in range(4):
                        njc = 4 * ig + 4
                        for jc in range(njc):
                            sp_, spb = (psC, "psC") if jc % 2 == 0 else (psD, "psD")
                            pp = jc % 2
                            O("pe", lambda e, jc=jc, ig=ig, sp_=sp_: e.matmul(sp_[:, 0:512], qkrot[:, T + jc * 128: T + (jc + 1) * 128], qkrot[:, ig * 512:(ig + 1) * 512], start=True, stop=True), reads=("qkrot", spb), writes=(spb,))
                            O("act", lambda e, sp_=sp_, jc=jc, pp=pp: e.activation(out=pblk[pp][:], in_=sp_[:, 0:512], func=AF.Exp, scale=rks[:, jc:jc + 1]), reads=(spb, "rks", f"pblk{pp}"), writes=(f"pblk{pp}",))
                            if jc >= 4 * ig:
                                off = jc * 128 - ig * 512
                                if off > 0:
                                    O("dve", lambda e, pp=pp, off=off: e.memset(pblk[pp][:, 0:off], 0.0), reads=(f"pblk{pp}",), writes=(f"pblk{pp}",))
                                O("dve", lambda e, pp=pp, off=off: e.tensor_tensor(out=pblk[pp][:, off:off + 128], in0=pblk[pp][:, off:off + 128], in1=masktb[:], op=ALU.mult), reads=(f"pblk{pp}", "masktb"), writes=(f"pblk{pp}",))
                            O("pe", lambda e, jc=jc, pp=pp, njc=njc: e.matmul(psE[:, 0:512], vtm[:, jc * 128:(jc + 1) * 128], pblk[pp][:], start=(jc == 0), stop=(jc == njc - 1)), reads=("vtm", f"pblk{pp}", "psE"), writes=("psE",), inc=(1 if jc == njc - 1 else 0))
                            O("pe", lambda e, jc=jc, pp=pp, njc=njc: e.matmul(psF[0:1, 0:512], onescol_bf[:], pblk[pp][:], start=(jc == 0), stop=(jc == njc - 1)), reads=("onescol", f"pblk{pp}", "psF"), writes=("psF",), inc=(1 if jc == njc - 1 else 0))
                        O("dve", lambda e: e.reciprocal(out=nrow[:], in_=psF[0:1, 0:512]), reads=("psF", "nrow"), writes=("nrow",))
                        O("pe", lambda e: e.matmul(psF[:, 0:512], onesrow[:], nrow[:], start=True, stop=True), reads=("nrow", "onesrow", "psF"), writes=("psF",))
                        O("dve", lambda e: e.tensor_copy(out=nb[:], in_=psF[:, 0:512]), reads=("psF", "nb"), writes=("nb",))
                        O("dve", lambda e, h=h, ig=ig: e.tensor_tensor(out=oall[:, h * T + ig * 512: h * T + (ig + 1) * 512], in0=psE[:, 0:512], in1=nb[:], op=ALU.mult), reads=("psE", "nb", "oall"), writes=("oall",))

                for mc in range(8):
                    slot = mc % 2
                    O("pool", lambda e, mc=mc, slot=slot: e.dma_start(out=wabf[slot][:].rearrange("p (c d) -> p c d", c=8), in_=d_wo[:, mc * 128:(mc + 1) * 128].rearrange("(c p) d -> p c d", p=128)), reads=(f"wabf{slot}",), writes=(f"wabf{slot}",), sem=("wa" if slot == 0 else "wb"), inc=16)
                    for tg in range(4):
                        for hh in range(8):
                            O("pe", lambda e, hh=hh, tg=tg, slot=slot: e.matmul(psA[:, 0:512], wabf[slot][:, hh * 128:(hh + 1) * 128], oall[:, hh * T + tg * 512: hh * T + (tg + 1) * 512], start=(hh == 0), stop=(hh == 7)), reads=(f"wabf{slot}", "oall", "psA"), writes=("psA",), inc=(1 if hh == 7 else 0))
                        O("dve", lambda e, mc=mc, tg=tg: e.tensor_tensor(out=xs[:, mc * T + tg * 512: mc * T + (tg + 1) * 512], in0=xs[:, mc * T + tg * 512: mc * T + (tg + 1) * 512], in1=psA[:, 0:512], op=ALU.add), reads=("psA", f"xs{mc}"), writes=(f"xs{mc}",))
                for fc in range(8):
                    O("sp", lambda e, fc=fc: e.dma_start(out=d_out[fc * 128:(fc + 1) * 128, :], in_=xs[:, fc * T:(fc + 1) * T]), reads=(f"xs{fc}",), writes=("out_dram",), sem="ds", inc=16)

        done = [False]

        def _stop(stage):
            if done[0]:
                return True
            if upto == stage:
                done[0] = True
            return False

        with contextlib.ExitStack() as f1:
            k1S_1 = f1.enter_context(nc.sbuf_tensor("k1S_1", [128, 8 * T], BF16))
            arow_1 = f1.enter_context(nc.sbuf_tensor("arow_1", [1, T], F32))
            emit_W()
            em.barrier()
            _stop("W")
            if upto.startswith("f1a:"):
                done[0] = False
            if not done[0]:
                lim = upto.split(":", 1)[1] if upto.startswith("f1a:") else ""
                emit_sub0(k1S_1, arow_1, d_x, None, copy_to=d_xwork, limit=lim)
                em.barrier()
                if upto.startswith("f1a"):
                    done[0] = True
            if not done[0]:
                emit_sub12(k1S_1, arow_1, d_xwork, "xwork")
                em.barrier()
                _stop("f1b")
        with contextlib.ExitStack() as p2:
            xs2 = p2.enter_context(nc.sbuf_tensor("xs2", [128, 8 * T], F32))
            for fc in range(8):
                em.buf.pop(f"xs{fc}", None)
            if not done[0]:
                emit_attn(xs2)
                em.barrier()
                _stop("attn")
        with contextlib.ExitStack() as f2:
            k1S_2 = f2.enter_context(nc.sbuf_tensor("k1S_2", [128, 8 * T], BF16))
            arow_2 = f2.enter_context(nc.sbuf_tensor("arow_2", [1, T], F32))
            for fc in range(8):
                em.buf.pop(f"xs{fc}", None)
            if not done[0]:
                emit_sub0(k1S_2, arow_2, d_out, "out_dram")
                em.barrier()
                _stop("f2a")
            if not done[0]:
                emit_sub12(k1S_2, arow_2, d_out, "out_dram")

        em.prog["sp"].append(([(s, em.cnt[s]) for s in ("wr", "ds") if em.cnt.get(s, 0) > 0], None, None, 0, None))
        em.finalize_check()

        eng_sems = [x for x in em.cnt if x not in DMA_SEMS]
        referenced = {x: set() for x in eng_sems}
        for e in ENG2SECT:
            for waits, _, _, _, _ in em.prog[e]:
                for s_, c_ in waits:
                    if s_ in referenced:
                        referenced[s_].add(c_)
        rank = {}
        for s_ in eng_sems:
            rank[s_] = {c_: i + 1 for i, c_ in enumerate(sorted(referenced[s_]))}

        with nc.Block() as block:
            def runner(eng_name):
                def go(handle):
                    for waits, fn, semn, inc, mycnt in em.prog[eng_name]:
                        for s, c in waits:
                            handle.wait_ge(semobjs[s], rank[s][c] if s in rank else c)
                        if fn is not None:
                            inst = fn(handle)
                            if semn is not None and (semn in DMA_SEMS or (mycnt is not None and mycnt[1] in referenced.get(semn, ()))):
                                inst.then_inc(semobjs[semn], inc)
                return go

            block.sync(runner("sp"))
            block.vector(runner("dve"))
            block.scalar(runner("act"))
            block.gpsimd(runner("pool"))
            block.tensor(runner("pe"))

    return nc


_NC_CACHE = [None]


def kernel(L0_attn_x, L0_attn_wq, L0_attn_wk, L0_attn_wv, L0_attn_wo,
           L0_ffn_w1, L0_ffn_conv_w, L0_ffn_conv_b, L0_ffn_w2):
    x = np.asarray(L0_attn_x, np.float32)
    wq = np.ascontiguousarray(np.asarray(L0_attn_wq, np.float32).reshape(DM, DM))
    wk = np.ascontiguousarray(np.asarray(L0_attn_wk, np.float32).reshape(DM, DM))
    wv = np.ascontiguousarray(np.asarray(L0_attn_wv, np.float32).reshape(DM, DM))
    wo = np.ascontiguousarray(np.asarray(L0_attn_wo, np.float32).reshape(DM, DM))
    w1t = np.ascontiguousarray(np.asarray(L0_ffn_w1, np.float32).T)
    w2t = np.ascontiguousarray(np.asarray(L0_ffn_w2, np.float32).T)
    cw = np.ascontiguousarray(np.asarray(L0_ffn_conv_w, np.float32)[:, 0, :]
                              .reshape(32, 128, 3).transpose(1, 0, 2).reshape(128, 96))
    cb = np.ascontiguousarray(np.asarray(L0_ffn_conv_b, np.float32).reshape(32, 128).T)

    if _NC_CACHE[0] is None:
        _NC_CACHE[0] = build()
    nc = _NC_CACHE[0]
    in_maps = []
    for b in range(8):
        in_maps.append({
            "x": np.ascontiguousarray(x[b].T),
            "wq": wq, "wk": wk, "wv": wv, "wo": wo,
            "w1t": w1t, "w2t": w2t, "cw": cw, "cb": cb,
        })
    res = run_bass_kernel_spmd(nc, in_maps, core_ids=list(range(8)))
    out = np.stack([np.ascontiguousarray(np.asarray(r["out"], np.float32).T) for r in res.results])
    return out


# version 13
# speedup vs baseline: 1.3255x; 1.3255x over previous
import contextlib
import numpy as np

import concourse.bass as bass
import concourse.mybir as mybir
from concourse.bass_utils import run_bass_kernel_spmd

F32 = mybir.dt.float32
BF16 = mybir.dt.bfloat16
F16 = mybir.dt.float16
AF = mybir.ActivationFunctionType
ALU = mybir.AluOpType
AX = mybir.AxisListType

M32 = 12582912.0
M64 = 6755399441055744.0
PI = 3.141592653589793
TWO_PI = 2.0 * PI
CLAMP = 3.14159

T = 2048
DM = 1024
DH = 4096
H = 8
GRP = 256
NG = T // GRP

ENG2SECT = {"pe": "tensor", "act": "scalar", "dve": "vector", "pool": "gpsimd", "sp": "sync"}
DMA_SEMS = frozenset(["d0", "d1", "d2", "d3", "ds", "wr", "wa", "wb"])


class Emit:

    def __init__(self, serialize=False):
        self.serialize = serialize
        self.last = None
        self.prog = {e: [] for e in ENG2SECT}
        self.cnt = {}
        self.seen = {e: {} for e in ENG2SECT}
        self.buf = {}
        self.pending = {e: [] for e in ENG2SECT}

    def op(self, eng, fn, reads=(), writes=(), sem=None, inc=1):
        semn = sem or eng
        self.cnt.setdefault(semn, 0)
        need = {}

        def dep(src):
            s, c = src
            if s in DMA_SEMS:
                c = self.cnt[s]
            if self.seen[eng].get(s, 0) >= c:
                return
            if need.get(s, 0) < c:
                need[s] = c

        for b in reads:
            r = self.buf.get(b)
            if r and r["w"]:
                dep(r["w"])
        for b in writes:
            r = self.buf.get(b)
            if r:
                if r["w"]:
                    dep(r["w"])
                for s, c in r["rs"].items():
                    dep((s, c))
        if self.serialize and self.last is not None:
            dep(self.last)
        waits = sorted(need.items())
        for s, c in waits:
            self.seen[eng][s] = c
        if inc:
            self.cnt[semn] += inc
            mycnt = (semn, self.cnt[semn])
        else:
            mycnt = None

        def rec(b, as_write, cntv):
            r = self.buf.setdefault(b, {"w": None, "rs": {}})
            if as_write:
                r["w"] = cntv
                r["rs"] = {}
            else:
                s, c = cntv
                if r["rs"].get(s, 0) < c:
                    r["rs"][s] = c

        if mycnt is None:
            for b in reads:
                self.pending[eng].append((b, False))
            for b in writes:
                self.pending[eng].append((b, True))
        else:
            for b, as_w in self.pending[eng]:
                rec(b, as_w, mycnt)
            self.pending[eng] = []
            for b in reads:
                rec(b, False, mycnt)
            for b in writes:
                rec(b, True, mycnt)
        if inc:
            self.last = mycnt
        self.prog[eng].append((waits, fn, semn if inc else None, inc, mycnt))

    def join(self, bufs):
        for b in bufs:
            r = self.buf.get(b)
            if r and r["w"]:
                s, _ = r["w"]
                r["w"] = (s, self.cnt[s])

    def barrier(self):
        snap = [(s, c) for s, c in self.cnt.items() if c > 0]
        for e in ENG2SECT:
            todo = [(s, c) for s, c in snap if self.seen[e].get(s, 0) < c]
            for s, c in todo:
                self.seen[e][s] = c
            if todo:
                self.prog[e].append((todo, None, None, 0, None))

    def finalize_check(self):
        for e, p in self.pending.items():
            assert not p, f"pending unresolved on {e}: {p[:4]}"
        for e in ENG2SECT:
            for waits, _, _, _, _ in self.prog[e]:
                for s, c in waits:
                    assert c <= self.cnt.get(s, 0), f"wait {s}>={c} never reached"


def build(upto="all", serialize=False):
    nc = bass.Bass("TRN2", target_bir_lowering=False)
    d_x = nc.dram_tensor("x", [DM, T], F32, kind="ExternalInput").ap()
    d_wq = nc.dram_tensor("wq", [DM, DM], F32, kind="ExternalInput").ap()
    d_wk = nc.dram_tensor("wk", [DM, DM], F32, kind="ExternalInput").ap()
    d_wv = nc.dram_tensor("wv", [DM, DM], F32, kind="ExternalInput").ap()
    d_wo = nc.dram_tensor("wo", [DM, DM], F32, kind="ExternalInput").ap()
    d_w1t = nc.dram_tensor("w1t", [DM, DH], F32, kind="ExternalInput").ap()
    d_w2t = nc.dram_tensor("w2t", [DH, DM], F32, kind="ExternalInput").ap()
    d_cw = nc.dram_tensor("cw", [128, 96], F32, kind="ExternalInput").ap()
    d_cb = nc.dram_tensor("cb", [128, 32], F32, kind="ExternalInput").ap()
    d_out = nc.dram_tensor("out", [DM, T], F32, kind="ExternalOutput").ap()
    d_xwork = nc.dram_tensor("xwork", [DM, T], F32, kind="Internal").ap()
    d_w1q = nc.dram_tensor("w1q", [DM, DH], BF16, kind="Internal").ap()
    d_w2q = nc.dram_tensor("w2q", [DH, DM], BF16, kind="Internal").ap()

    em = Emit(serialize=serialize)
    O = em.op
    _uid = [0]

    def _un(n):
        _uid[0] += 1
        return f"{n}_{_uid[0]}"

    with contextlib.ExitStack() as top:
        sb = lambda n, sh, dt: top.enter_context(nc.sbuf_tensor(n, sh, dt))
        ones_bf = sb("ones_bf", [128, 128], BF16)
        ones_f = sb("ones_f", [128, 128], F32)
        onesrow = sb("onesrow", [1, 128], F32)
        c127row = sb("c127row", [1, 128], F32)
        onescol_bf = sb("onescol_bf", [128, 1], BF16)
        identf = sb("identf", [128, 128], F32)
        identb = sb("identb", [128, 128], BF16)
        masktb = sb("masktb", [128, 128], BF16)
        selE = sb("selE", [128, 128], BF16)
        selO = sb("selO", [128, 128], BF16)
        selscr = sb("selscr", [128, 128], F32)
        iot = sb("iot", [128, 128], F32)
        wcol = sb("wcol", [128, 1], F32)
        mcol = sb("mcol", [128, 1], F32)
        nmcol = sb("nmcol", [128, 1], F32)
        eps5 = sb("eps5", [128, 1], F32)
        eps6 = sb("eps6", [128, 1], F32)
        c128 = sb("c128", [128, 1], F32)
        m1col = sb("m1col", [128, 1], F32)
        lncol = sb("lncol", [1, 1], F32)
        phc = sb("phc", [128, 1], F32)
        phs = sb("phs", [128, 1], F32)
        phcL = sb("phcL", [128, 1], F32)
        phsL = sb("phsL", [128, 1], F32)
        lnL = sb("lnL", [1, 1], F32)
        mwc = sb("mwc", [128, 4], F32)
        mwa = sb("mwa", [128, 2], F32)
        cwS = sb("cwS", [128, 96], F32)
        cbS = sb("cbS", [128, 32], F32)
        rowA = sb("rowA", [1, 512], F32)
        rowB = sb("rowB", [1, 512], F32)
        rowC = sb("rowC", [1, 512], F32)
        rowD = sb("rowD", [1, 512], F32)
        rowE = sb("rowE", [1, 512], F32)
        freq2 = sb("freq2", [1, 128], F32)

        psA = top.enter_context(nc.psum_tensor("psA", [128, 512], F32))
        psB = top.enter_context(nc.psum_tensor("psB", [128, 512], F32))
        psC = top.enter_context(nc.psum_tensor("psC", [128, 512], F32))
        psD = top.enter_context(nc.psum_tensor("psD", [128, 512], F32))
        psE = top.enter_context(nc.psum_tensor("psE", [128, 512], F32))
        psF = top.enter_context(nc.psum_tensor("psF", [128, 512], F32))
        psS = top.enter_context(nc.psum_tensor("psS", [128, 384], F32))
        pst = top.enter_context(nc.psum_tensor("pst", [128, 128], BF16))

        semobjs = {}
        for s in ["pe", "act", "dve", "pool", "sp", "d0", "d1", "d2", "d3", "ds", "wr", "wa", "wb"]:
            semobjs[s] = top.enter_context(nc.semaphore("sem_" + s))
            em.cnt.setdefault(s, 0)

        O("dve", lambda e: e.memset(ones_bf[:], 1.0), writes=("ones_bf",))
        O("dve", lambda e: e.memset(ones_f[:], 1.0), writes=("ones_f",))
        O("dve", lambda e: e.memset(onesrow[:], 1.0), writes=("onesrow",))
        O("dve", lambda e: e.memset(c127row[:], 127.0), writes=("c127row",))
        O("dve", lambda e: e.memset(onescol_bf[:], 1.0), writes=("onescol",))
        O("dve", lambda e: e.memset(wcol[:], 1.0), writes=("wcol",))
        O("dve", lambda e: e.tensor_scalar(out=wcol[:], in0=wcol[:], scalar1=M64, scalar2=-M64, op0=ALU.add, op1=ALU.add), reads=("wcol",), writes=("wcol",))
        O("dve", lambda e: e.tensor_scalar(out=mcol[:], in0=wcol[:], scalar1=M64 - M32, scalar2=M32, op0=ALU.mult, op1=ALU.add), reads=("wcol",), writes=("mcol",))
        O("dve", lambda e: e.tensor_scalar(out=nmcol[:], in0=mcol[:], scalar1=-1.0, scalar2=None, op0=ALU.mult), reads=("mcol",), writes=("nmcol",))
        O("dve", lambda e: e.memset(eps5[:], 1e-5), writes=("eps5",))
        O("dve", lambda e: e.memset(eps6[:], 1e-6), writes=("eps6",))
        O("dve", lambda e: e.memset(c128[:], 1.0 / 128.0), writes=("c128",))
        O("dve", lambda e: e.memset(m1col[:], -1.0), writes=("m1col",))
        _LN = -float(np.log(np.float64(10000.0))) / 64.0
        _LNH = float(np.float32(_LN))
        _LNL = float(np.float64(_LN) - np.float64(_LNH))
        _PH = float(np.float32(PI / 2.0))
        _PL = float(np.float64(PI / 2.0) - np.float64(_PH))
        O("dve", lambda e: e.memset(lncol[:], _LNH), writes=("lncol",))
        O("dve", lambda e: e.memset(lnL[:], _LNL), writes=("lnL",))
        O("dve", lambda e: e.memset(phc[0:64, :], _PH), writes=("phc",))
        O("dve", lambda e: e.memset(phc[64:128, :], 0.0), reads=("phc",), writes=("phc",))
        O("dve", lambda e: e.memset(phcL[0:64, :], _PL), writes=("phcL",))
        O("dve", lambda e: e.memset(phcL[64:128, :], 0.0), reads=("phcL",), writes=("phcL",))
        O("dve", lambda e: e.memset(phs[0:64, :], 0.0), writes=("phs",))
        O("dve", lambda e: e.memset(phs[64:128, :], _PH), reads=("phs",), writes=("phs",))
        O("dve", lambda e: e.memset(phsL[0:64, :], 0.0), writes=("phsL",))
        O("dve", lambda e: e.memset(phsL[64:128, :], _PL), reads=("phsL",), writes=("phsL",))
        O("pool", lambda e: e.iota(out=iot[:], pattern=[[1, 128]], base=0, channel_multiplier=-1, allow_small_or_imprecise_dtypes=True), writes=("iot",))
        O("dve", lambda e: e.tensor_scalar(out=identf[:], in0=iot[:], scalar1=0.0, scalar2=None, op0=ALU.is_equal), reads=("iot",), writes=("identf",))
        O("dve", lambda e: e.tensor_copy(out=identb[:], in_=identf[:]), reads=("identf",), writes=("identb",))
        O("dve", lambda e: e.tensor_scalar(out=masktb[:], in0=iot[:], scalar1=0.0, scalar2=None, op0=ALU.is_ge), reads=("iot",), writes=("masktb",))
        O("dve", lambda e: e.memset(selE[:], 0.0), writes=("selE",))
        O("dve", lambda e: e.memset(selO[:], 0.0), writes=("selO",))
        O("dve", lambda e: e.tensor_scalar(out=selscr[:], in0=iot[:], scalar1=-64.0, scalar2=None, op0=ALU.is_equal), reads=("iot",), writes=("selscr",))
        O("dve", lambda e: e.tensor_tensor(out=selE[:, 0:64], in0=identf[:, 0:64], in1=selscr[:, 0:64], op=ALU.subtract), reads=("identf", "selscr", "selE"), writes=("selE",))
        O("dve", lambda e: e.tensor_scalar(out=selscr[:], in0=iot[:], scalar1=64.0, scalar2=None, op0=ALU.is_equal), reads=("iot", "selscr"), writes=("selscr",))
        O("dve", lambda e: e.tensor_tensor(out=selO[:, 64:128], in0=identf[:, 64:128], in1=selscr[:, 64:128], op=ALU.add), reads=("identf", "selscr", "selO"), writes=("selO",))
        O("pool", lambda e: e.iota(out=freq2[:], pattern=[[0, 2], [1, 64]], base=0, channel_multiplier=0, allow_small_or_imprecise_dtypes=True), writes=("freq2",))
        O("dve", lambda e: e.tensor_scalar(out=rowE[0:1, 0:128], in0=freq2[:], scalar1=lnL[:], scalar2=None, op0=ALU.mult), reads=("freq2", "lnL", "rowE"), writes=("rowE",))
        O("dve", lambda e: e.tensor_scalar(out=freq2[:], in0=freq2[:], scalar1=lncol[:], scalar2=None, op0=ALU.mult), reads=("freq2", "lncol"), writes=("freq2",))
        O("dve", lambda e: e.tensor_tensor(out=freq2[:], in0=freq2[:], in1=rowE[0:1, 0:128], op=ALU.add), reads=("freq2", "rowE"), writes=("freq2",))
        O("act", lambda e: e.activation(out=freq2[:], in_=freq2[:], func=AF.Exp), reads=("freq2",), writes=("freq2",))
        O("sp", lambda e: e.dma_start(out=cwS[:], in_=d_cw[:]), writes=("cwS",), sem="d2", inc=16)
        O("sp", lambda e: e.dma_start(out=cbS[:], in_=d_cb[:]), writes=("cbS",), sem="d2", inc=16)
        em.join(("cwS", "cbS"))

        def rne_round(src_ap, src_bufs, dst_ap, dst_buf, eng="dve"):
            O(eng, lambda e: e.tensor_scalar(out=dst_ap, in0=src_ap, scalar1=mcol[:], scalar2=nmcol[:], op0=ALU.add, op1=ALU.add), reads=tuple(src_bufs) + ("mcol", "nmcol", dst_buf), writes=(dst_buf,))

        def emit_W():
            with contextlib.ExitStack() as ws:
                wsb = lambda n, sh, dt: ws.enter_context(nc.sbuf_tensor(_un(n), sh, dt))
                wstr = [wsb(f"wstr{i}", [128, DH], F32) for i in range(4)]
                wqb = [wsb("wqb0", [128, DH], BF16), wsb("wqb1", [128, DH], BF16)]
                wabs = wsb("wabs", [128, DH], BF16)
                wcs = wsb("wcs", [128, 64], F32)
                wtT = [wsb("wtT0", [128, DH], F32), wsb("wtT1", [128, DH], F32)]

                pieces = [("w1", c, DH) for c in range(8)] + [("w2", c, DM) for c in range(32)]

                def src_ap(wn, c, wid):
                    d = d_w1t if wn == "w1" else d_w2t
                    return d[c * 128:(c + 1) * 128, 0:wid]

                for ci, (wn, c, wid) in enumerate(pieces):
                    p = ci % 4
                    O("sp", lambda e, wn=wn, c=c, wid=wid, p=p: e.dma_start(out=wstr[p][:, 0:wid], in_=src_ap(wn, c, wid)), reads=(f"wstr{p}",), writes=(f"wstr{p}",), sem=f"d{p}", inc=16)
                    if ci % 2 == 0:
                        O("act", lambda e, p=p, wid=wid, ci=ci: e.activation(out=wabs[:, 0:wid], in_=wstr[p][:, 0:wid], func=AF.Abs, accum_out=wcs[:, ci:ci + 1]), reads=(f"wstr{p}", "wabs"), writes=(f"wcs{ci}", "wabs"))
                    else:
                        O("dve", lambda e, p=p, wid=wid, ci=ci: e.tensor_reduce(out=wcs[:, ci:ci + 1], in_=wstr[p][:, 0:wid], axis=AX.X, op=ALU.add, apply_absolute_value=True), reads=(f"wstr{p}",), writes=(f"wcs{ci}",))
                O("pe", lambda e: e.matmul(psS[:, 0:40], ones_f[:], wcs[:, 0:40], start=True, stop=True), reads=tuple(f"wcs{ci}" for ci in range(40)) + ("ones_f", "psSa"), writes=("psSa",))
                O("dve", lambda e: e.tensor_reduce(out=mwc[:, 0:1], in_=psS[:, 0:8], axis=AX.X, op=ALU.add), reads=("psSa", "mwc"), writes=("mwc",))
                O("dve", lambda e: e.tensor_reduce(out=mwc[:, 1:2], in_=psS[:, 8:40], axis=AX.X, op=ALU.add), reads=("psSa", "mwc"), writes=("mwc",))
                for j in range(2):
                    O("dve", lambda e, j=j: e.tensor_scalar(out=mwc[:, j:j + 1], in0=mwc[:, j:j + 1], scalar1=1.0 / (DH * DM), scalar2=1e-5, op0=ALU.mult, op1=ALU.max), reads=("mwc",), writes=("mwc",))
                    O("dve", lambda e, j=j: e.reciprocal(out=mwc[:, 2 + j:3 + j], in_=mwc[:, j:j + 1]), reads=("mwc",), writes=("mwc",))
                O("dve", lambda e: e.tensor_scalar(out=mwa[:, 0:1], in0=mwc[:, 0:1], scalar1=1.0 / 127.0, scalar2=None, op0=ALU.mult), reads=("mwc",), writes=("mwa",))
                O("dve", lambda e: e.tensor_scalar(out=mwa[:, 1:2], in0=mwc[:, 1:2], scalar1=0.5 / 127.0, scalar2=None, op0=ALU.mult), reads=("mwc", "mwa"), writes=("mwa",))

                for ci, (wn, c, wid) in enumerate(pieces):
                    p = ci % 4
                    q = ci % 2
                    j = 2 if wn == "w1" else 3
                    O("sp", lambda e, wn=wn, c=c, wid=wid, p=p: e.dma_start(out=wstr[p][:, 0:wid], in_=src_ap(wn, c, wid)), reads=(f"wstr{p}",), writes=(f"wstr{p}",), sem=f"d{p}", inc=16)
                    O("act", lambda e, p=p, q=q, wid=wid, j=j: e.activation(out=wtT[q][:, 0:wid], in_=wstr[p][:, 0:wid], func=AF.Copy, scale=mwc[:, j:j + 1]), reads=(f"wstr{p}", "mwc", f"wtT{q}"), writes=(f"wtT{q}",))
                    rne_round(wtT[q][:, 0:wid], (f"wtT{q}",), wtT[q][:, 0:wid], f"wtT{q}")
                    O("dve", lambda e, q=q, wid=wid: e.tensor_scalar(out=wqb[q][:, 0:wid], in0=wtT[q][:, 0:wid], scalar1=-1.0, scalar2=1.0, op0=ALU.max, op1=ALU.min), reads=(f"wtT{q}", f"wqb{q}"), writes=(f"wqb{q}",))
                    if wn == "w1":
                        O("sp", lambda e, c=c, q=q: e.dma_start(out=d_w1q[c * 128:(c + 1) * 128, :], in_=wqb[q][:, 0:DH]), reads=(f"wqb{q}",), writes=("w1q_dram",), sem="ds", inc=16)
                    else:
                        O("sp", lambda e, c=c, q=q, wid=wid: e.dma_start(out=d_w2q[c * 128:(c + 1) * 128, :], in_=wqb[q][:, 0:wid]), reads=(f"wqb{q}",), writes=(f"w2q_dram{c}",), sem="ds", inc=16)

        def emit_sub0(k1S, arow, src_dram, src_buf, copy_to=None, copy_buf="xwork", xs_ext=None, limit=""):
            with contextlib.ExitStack() as s0:
                zb = lambda n, sh, dt: s0.enter_context(nc.sbuf_tensor(_un(n), sh, dt))
                xs = xs_ext if xs_ext is not None else zb("xs0t", [128, 8 * T], F32)
                xbF = zb("xbF", [128, 8 * 512], BF16)
                xqF = zb("xqF", [128, 8 * 512], BF16)
                runmx = zb("runmx", [128, 512], BF16)
                runmn = zb("runmn", [128, 512], BF16)
                cst = zb("cst", [128, 16], F32)
                asb = zb("asb", [128, 512], F32)
                bbt = zb("bbt", [128, 512], F32)
                xa = [zb("xa0", [128, 512], F32), zb("xa1", [128, 512], F32)]
                tts = [zb("tts0", [128, 512], F32), zb("tts1", [128, 512], F32)]

                if src_dram is not None:
                    rd = (src_buf,) if src_buf else ()
                    for fc in range(8):
                        O("sp", lambda e, fc=fc: e.dma_start(out=xs[:, fc * T:(fc + 1) * T], in_=src_dram[fc * 128:(fc + 1) * 128, :]), reads=rd, writes=(f"xs{fc}",), sem=f"d{fc % 4}", inc=16)
                    em.join(tuple(f"xs{fc}" for fc in range(8)))
                if copy_to is not None:
                    for fc in range(8):
                        O("sp", lambda e, fc=fc: e.dma_start(out=copy_to[fc * 128:(fc + 1) * 128, :], in_=xs[:, fc * T:(fc + 1) * T]), reads=(f"xs{fc}",), writes=(copy_buf,), sem="ds", inc=16)

                if limit == "loads":
                    return
                _ntg = 4
                if limit.startswith("n"):
                    _ntg = int(limit[1])
                for tg in range(_ntg):
                    t0 = tg * 512
                    for fc in range(8):
                        xsl = xs[:, fc * T + t0: fc * T + t0 + 512]
                        xbl = xbF[:, fc * 512:(fc + 1) * 512]
                        O("act", lambda e, xsl=xsl, xbl=xbl: e.activation(out=xbl, in_=xsl, func=AF.Copy), reads=(f"xs{fc}", f"xb{fc}"), writes=(f"xb{fc}",))
                        O("act", lambda e, fc=fc, xbl=xbl: e.activation(out=xqF[:, fc * 512:(fc + 1) * 512], in_=xbl, func=AF.Square), reads=(f"xb{fc}", f"xq{fc}"), writes=(f"xq{fc}",))
                        if fc == 0:
                            O("dve", lambda e, xbl=xbl: e.tensor_copy(out=runmx[:], in_=xbl), reads=(f"xb{fc}", "runmx"), writes=("runmx",))
                            O("dve", lambda e, xbl=xbl: e.tensor_copy(out=runmn[:], in_=xbl), reads=(f"xb{fc}", "runmn"), writes=("runmn",))
                        else:
                            O("dve", lambda e, xbl=xbl: e.tensor_tensor(out=runmx[:], in0=runmx[:], in1=xbl, op=ALU.max), reads=(f"xb{fc}", "runmx"), writes=("runmx",))
                            O("dve", lambda e, xbl=xbl: e.tensor_tensor(out=runmn[:], in0=runmn[:], in1=xbl, op=ALU.min), reads=(f"xb{fc}", "runmn"), writes=("runmn",))
                    for ti in range(4):
                        for fc in range(8):
                            O("pe", lambda e, ti=ti, fc=fc: e.matmul(psS[:, ti:ti + 1], xbF[:, fc * 512 + ti * 128: fc * 512 + (ti + 1) * 128], onescol_bf[:], start=(fc == 0), stop=(fc == 7)), reads=(f"xb{fc}", "onescol", "psSa"), writes=("psSa",), inc=(1 if fc == 7 else 0))
                    for ti in range(4):
                        for fc in range(8):
                            O("pe", lambda e, ti=ti, fc=fc: e.matmul(psS[:, 4 + ti:5 + ti], xqF[:, fc * 512 + ti * 128: fc * 512 + (ti + 1) * 128], onescol_bf[:], start=(fc == 0), stop=(fc == 7)), reads=(f"xq{fc}", "onescol", "psSa"), writes=("psSa",), inc=(1 if fc == 7 else 0))
                    cmean = cst[:, 0:4]
                    cvar = cst[:, 4:8]
                    ca = cst[:, 12:16]
                    O("dve", lambda e, cmean=cmean: e.tensor_scalar(out=cmean, in0=psS[:, 0:4], scalar1=1.0 / DM, scalar2=None, op0=ALU.mult), reads=("psSa", "cst"), writes=("cst",))
                    O("dve", lambda e, cmean=cmean, cvar=cvar: e.tensor_mul(out=cvar, in0=cmean, in1=cmean), reads=("cst",), writes=("cst",))
                    O("dve", lambda e, cvar=cvar: e.scalar_tensor_tensor(out=cvar, in0=psS[:, 4:8], scalar=1.0 / DM, in1=cvar, op0=ALU.mult, op1=ALU.subtract), reads=("psSa", "cst"), writes=("cst",))
                    O("act", lambda e, cvar=cvar: e.activation(out=cvar, in_=cvar, func=AF.Sqrt, bias=eps5[:]), reads=("cst", "eps5"), writes=("cst",))
                    O("dve", lambda e, cvar=cvar: e.reciprocal(out=cvar, in_=cvar), reads=("cst",), writes=("cst",))
                    for ti in range(4):
                        O("pe", lambda e, ti=ti: e.transpose(pst[:], runmx[:, ti * 128:(ti + 1) * 128], identb[:]), reads=("runmx", "identb", "pst"), writes=("pst",))
                        O("dve", lambda e, ti=ti: e.tensor_reduce(out=cst[:, 12 + ti:13 + ti], in_=pst[:], axis=AX.X, op=ALU.max), reads=("pst", "cst"), writes=("cst",))
                    for ti in range(4):
                        O("pe", lambda e, ti=ti: e.transpose(pst[:], runmn[:, ti * 128:(ti + 1) * 128], identb[:]), reads=("runmn", "identb", "pst"), writes=("pst",))
                        O("dve", lambda e, ti=ti: e.tensor_reduce(out=cst[:, 8 + ti:9 + ti], in_=pst[:], axis=AX.X, op=ALU.min), reads=("pst", "cst"), writes=("cst",))
                    O("dve", lambda e, ca=ca, cmean=cmean: e.tensor_sub(out=ca, in0=ca, in1=cmean), reads=("cst",), writes=("cst",))
                    O("dve", lambda e, cmean=cmean: e.tensor_sub(out=cst[:, 8:12], in0=cmean, in1=cst[:, 8:12]), reads=("cst",), writes=("cst",))
                    O("dve", lambda e, ca=ca: e.tensor_tensor(out=ca, in0=ca, in1=cst[:, 8:12], op=ALU.max), reads=("cst",), writes=("cst",))
                    O("dve", lambda e, ca=ca, cvar=cvar: e.tensor_tensor(out=ca, in0=ca, in1=cvar, op=ALU.mult), reads=("cst",), writes=("cst",))
                    O("dve", lambda e, ca=ca: e.tensor_scalar(out=ca, in0=ca, scalar1=1e-5, scalar2=None, op0=ALU.max), reads=("cst",), writes=("cst",))
                    O("dve", lambda e: e.tensor_scalar(out=cst[:, 8:12], in0=cst[:, 12:16], scalar1=mwa[:, 0:1], scalar2=None, op0=ALU.mult), reads=("cst", "mwa"), writes=("cst",))
                    for ti in range(4):
                        O("pe", lambda e, ti=ti: e.matmul(psF[0:1, ti * 128:(ti + 1) * 128], cst[:, 8 + ti:9 + ti], identf[:], start=True, stop=True), reads=("cst", "identf", "psF"), writes=("psF",))
                    O("dve", lambda e, t0=t0: e.tensor_copy(out=arow[0:1, t0:t0 + 512], in_=psF[0:1, 0:512]), reads=("psF", "arow"), writes=("arow",))
                    O("dve", lambda e: e.reciprocal(out=cst[:, 8:12], in_=cst[:, 12:16]), reads=("cst",), writes=("cst",))
                    O("dve", lambda e: e.tensor_scalar(out=cst[:, 8:12], in0=cst[:, 8:12], scalar1=127.0, scalar2=None, op0=ALU.mult), reads=("cst",), writes=("cst",))
                    O("dve", lambda e, cvar=cvar: e.tensor_tensor(out=cst[:, 8:12], in0=cst[:, 8:12], in1=cvar, op=ALU.mult), reads=("cst",), writes=("cst",))
                    O("dve", lambda e, cmean=cmean: e.tensor_tensor(out=cst[:, 12:16], in0=cmean, in1=cst[:, 8:12], op=ALU.mult), reads=("cst",), writes=("cst",))
                    for ti in range(4):
                        O("pe", lambda e, ti=ti: e.matmul(psE[0:1, ti * 128:(ti + 1) * 128], cst[:, 8 + ti:9 + ti], identf[:], start=True, stop=True), reads=("cst", "identf", "psE"), writes=("psE",))
                    O("dve", lambda e: e.tensor_copy(out=rowC[0:1, 0:512], in_=psE[0:1, 0:512]), reads=("psE", "rowCs"), writes=("rowCs",))
                    for ti in range(4):
                        O("pe", lambda e, ti=ti: e.matmul(psF[0:1, ti * 128:(ti + 1) * 128], cst[:, 12 + ti:13 + ti], identf[:], start=True, stop=True), reads=("cst", "identf", "psF"), writes=("psF",))
                    O("dve", lambda e: e.tensor_copy(out=rowD[0:1, 0:512], in_=psF[0:1, 0:512]), reads=("psF", "rowDs"), writes=("rowDs",))
                    O("pe", lambda e: e.matmul(psE[:, 0:512], onesrow[:], rowC[0:1, 0:512], start=True, stop=True), reads=("rowCs", "onesrow", "psE"), writes=("psE",))
                    O("dve", lambda e: e.tensor_copy(out=asb[:], in_=psE[:, 0:512]), reads=("psE", "asb"), writes=("asb",))
                    O("pe", lambda e: e.matmul(psF[:, 0:512], onesrow[:], rowD[0:1, 0:512], start=True, stop=True), reads=("rowDs", "onesrow", "psF"), writes=("psF",))
                    O("dve", lambda e: e.tensor_copy(out=bbt[:], in_=psF[:, 0:512]), reads=("psF", "bbt"), writes=("bbt",))
                    for fc in range(8):
                        xsl = xs[:, fc * T + t0: fc * T + t0 + 512]
                        p2_ = fc % 2
                        O("pool", lambda e, xsl=xsl, p2_=p2_: e.tensor_mul(out=xa[p2_][:], in0=xsl, in1=asb[:]), reads=(f"xs{fc}", "asb", f"xa{p2_}"), writes=(f"xa{p2_}",))
                        O("dve", lambda e, p2_=p2_: e.tensor_sub(out=tts[p2_][:], in0=xa[p2_][:], in1=bbt[:]), reads=(f"xa{p2_}", "bbt", f"tts{p2_}"), writes=(f"tts{p2_}",))
                        rne_round(tts[p2_][:], (f"tts{p2_}",), k1S[:, fc * T + t0: fc * T + t0 + 512], f"k1_{fc}")

        def emit_sub12(k1S, arow, dst_dram, dst_buf, w1q_ext=None):
            with contextlib.ExitStack() as s1:
                zb = lambda n, sh, dt: s1.enter_context(nc.sbuf_tensor(_un(n), sh, dt))
                w1qbf = w1q_ext if w1q_ext is not None else zb("w1qbf", [128, 8 * DH], BF16)
                diag3 = zb("diag3", [128, 32 * 384], BF16)
                uS = zb("uS", [128, 32 * GRP], F16)
                k2S = zb("k2S", [128, 32 * GRP], BF16)
                hbf = [zb(f"hbf{i}", [128, 260], BF16) for i in range(4)]
                ab = [zb("ab0", [128, 260], F32), zb("ab1", [128, 260], F32)]
                s2b2 = zb("s2b2", [128, 512], F32)
                betb = zb("betb", [128, 256], F32)
                ufmx = zb("ufmx", [128, 512], F16)
                ufmn = zb("ufmn", [128, 512], F16)
                urun = zb("urun", [128, 512], F32)
                colsc2 = zb("colsc2", [128, 4], F32)
                t32 = [zb("t32a", [128, 512], F32), zb("t32b", [128, 512], F32)]
                tmpe = [zb("tmpe0", [128, 256], F32), zb("tmpe1", [128, 256], F32)]
                w2t_ = [zb(f"w2t{i}", [128, DM], BF16) for i in range(4)]

                if w1q_ext is None:
                    for fc in range(8):
                        O("sp", lambda e, fc=fc: e.dma_start(out=w1qbf[:, fc * DH:(fc + 1) * DH], in_=d_w1q[fc * 128:(fc + 1) * 128, :]), reads=("w1q_dram",), writes=(f"w1qbf{fc}",), sem=f"d{fc % 2}", inc=16)
                    em.join(tuple(f"w1qbf{fc}" for fc in range(8)))
                for dc in range(32):
                    for k in range(3):
                        O("dve", lambda e, dc=dc, k=k: e.tensor_scalar(out=diag3[:, dc * 384 + k * 128: dc * 384 + (k + 1) * 128], in0=identf[:], scalar1=cwS[:, dc * 3 + k: dc * 3 + k + 1], scalar2=None, op0=ALU.mult), reads=("identf", "cwS", f"diag{dc}"), writes=(f"diag{dc}",))

                acc_ps = [psB, psB, psC, psC, psD, psD, psE, psE]
                acc_off = [0, 256, 0, 256, 0, 256, 0, 256]
                acc_nm = ["psB", "psB", "psC", "psC", "psD", "psD", "psE", "psE"]
                hring = [(psE, "psE"), (psF, "psF"), (psB, "psB")]

                def geo(g):
                    g0 = g * GRP
                    lo = max(g0 - 1, 0)
                    hi = min(g0 + GRP + 1, T)
                    return g0, lo, hi, hi - lo, g0 - lo

                def emit_abg(g):
                    g0, lo, hi, n, goff = geo(g)
                    a = g % 2
                    O("pe", lambda e, lo=lo, n=n: e.matmul(psA[:, 0:n], onesrow[:], arow[0:1, lo:lo + n], start=True, stop=True), reads=("arow", "onesrow", "psA"), writes=("psA",))
                    O("dve", lambda e, n=n, a=a: e.tensor_copy(out=ab[a][:, 0:n], in_=psA[:, 0:n]), reads=("psA", f"ab{a}"), writes=(f"ab{a}",))

                def emit_h(g, dc):
                    g0, lo, hi, n, goff = geo(g)
                    hp, hpb = hring[dc % 3]
                    r = dc % 4
                    a = g % 2
                    for fc in range(8):
                        O("pe", lambda e, fc=fc, dc=dc, hp=hp, lo=lo, hi=hi, n=n: e.matmul(hp[:, 0:n], w1qbf[:, fc * DH + dc * 128: fc * DH + (dc + 1) * 128], k1S[:, fc * T + lo: fc * T + hi], start=(fc == 0), stop=(fc == 7)), reads=(f"w1qbf{fc}", f"k1_{fc}", hpb), writes=(hpb,), inc=(1 if fc == 7 else 0))
                    O("dve", lambda e, hp=hp, r=r, n=n, a=a: e.tensor_tensor(out=hbf[r][:, 0:n], in0=hp[:, 0:n], in1=ab[a][:, 0:n], op=ALU.mult), reads=(hpb, f"ab{a}", f"hbf{r}"), writes=(f"hbf{r}",))

                def emit_conv(g, dc):
                    g0, lo, hi, n, goff = geo(g)
                    cp, cpb = (psC, "psC") if dc % 2 == 0 else (psD, "psD")
                    r = dc % 4
                    taps = []
                    for k in (1, 0, 2):
                        j0 = goff + (k - 1)
                        lo_in = max(j0, 0)
                        hi_in = min(j0 + GRP, n)
                        o0 = lo_in - j0
                        o1 = o0 + (hi_in - lo_in)
                        taps.append((k, lo_in, hi_in, o0, o1))
                    for idx, (k, lo_in, hi_in, o0, o1) in enumerate(taps):
                        O("pe", lambda e, dc=dc, k=k, cp=cp, r=r, lo_in=lo_in, hi_in=hi_in, o0=o0, o1=o1, idx=idx: e.matmul(cp[:, o0:o1], diag3[:, dc * 384 + k * 128: dc * 384 + (k + 1) * 128], hbf[r][:, lo_in:hi_in], start=(idx == 0), stop=(idx == 2)), reads=(f"diag{dc}", f"hbf{r}", cpb), writes=(cpb,), inc=(1 if idx == 2 else 0))
                    O("act", lambda e, cp=cp, dc=dc: e.activation(out=uS[:, dc * GRP:(dc + 1) * GRP], in_=cp[:, 0:GRP], func=AF.Silu, bias=cbS[:, dc:dc + 1]), reads=(cpb, "cbS", f"uS{dc}"), writes=(f"uS{dc}",))
                    if dc == 1:
                        O("dve", lambda e: e.tensor_copy(out=ufmx[:], in_=uS[:, 0:2 * GRP]), reads=("uS0", "uS1", "ufmx"), writes=("ufmx",))
                        O("dve", lambda e: e.tensor_copy(out=ufmn[:], in_=uS[:, 0:2 * GRP]), reads=("uS0", "uS1", "ufmn"), writes=("ufmn",))
                    elif dc % 2 == 1:
                        O("dve", lambda e, dc=dc: e.tensor_tensor(out=ufmx[:], in0=ufmx[:], in1=uS[:, (dc - 1) * GRP:(dc + 1) * GRP], op=ALU.max), reads=(f"uS{dc - 1}", f"uS{dc}", "ufmx"), writes=("ufmx",))
                        O("dve", lambda e, dc=dc: e.tensor_tensor(out=ufmn[:], in0=ufmn[:], in1=uS[:, (dc - 1) * GRP:(dc + 1) * GRP], op=ALU.min), reads=(f"uS{dc - 1}", f"uS{dc}", "ufmn"), writes=("ufmn",))

                def emit_body(g, dc_from, dc_to):
                    for dc in range(dc_from, dc_to):
                        emit_h(g, dc)
                        if dc - 2 >= 0:
                            emit_conv(g, dc - 2)
                    if dc_to == 32:
                        emit_conv(g, 30)
                        emit_conv(g, 31)

                def emit_tail(g):
                    g0, lo, hi, n, goff = geo(g)
                    O("dve", lambda e: e.tensor_tensor(out=urun[:, 0:256], in0=ufmx[:, 0:256], in1=ufmx[:, 256:512], op=ALU.max), reads=("ufmx", "urun"), writes=("urun",))
                    O("dve", lambda e: e.tensor_tensor(out=urun[:, 256:512], in0=ufmn[:, 0:256], in1=ufmn[:, 256:512], op=ALU.min), reads=("ufmn", "urun"), writes=("urun",))
                    for ti in range(2):
                        O("pe", lambda e, ti=ti: e.transpose(psS[:, 0:128], urun[:, ti * 128:(ti + 1) * 128], identf[:]), reads=("urun", "identf", "psSa"), writes=("psSa",))
                        O("dve", lambda e, ti=ti: e.tensor_reduce(out=colsc2[:, ti:ti + 1], in_=psS[:, 0:128], axis=AX.X, op=ALU.max), reads=("psSa", "colsc2"), writes=("colsc2",))
                    for ti in range(2):
                        O("pe", lambda e, ti=ti: e.transpose(psS[:, 0:128], urun[:, 256 + ti * 128: 256 + (ti + 1) * 128], identf[:]), reads=("urun", "identf", "psSa"), writes=("psSa",))
                        O("dve", lambda e, ti=ti: e.tensor_reduce(out=colsc2[:, 2 + ti:3 + ti], in_=psS[:, 0:128], axis=AX.X, op=ALU.min), reads=("psSa", "colsc2"), writes=("colsc2",))
                    O("dve", lambda e: e.tensor_scalar(out=colsc2[:, 2:4], in0=colsc2[:, 2:4], scalar1=-1.0, scalar2=None, op0=ALU.mult), reads=("colsc2",), writes=("colsc2",))
                    O("dve", lambda e: e.tensor_tensor(out=colsc2[:, 0:2], in0=colsc2[:, 0:2], in1=colsc2[:, 2:4], op=ALU.max), reads=("colsc2",), writes=("colsc2",))
                    O("dve", lambda e: e.tensor_scalar(out=colsc2[:, 0:2], in0=colsc2[:, 0:2], scalar1=1e-5, scalar2=None, op0=ALU.max), reads=("colsc2",), writes=("colsc2",))
                    for ti in range(2):
                        O("pe", lambda e, ti=ti: e.matmul(psF[0:1, ti * 128:(ti + 1) * 128], colsc2[:, ti:ti + 1], identf[:], start=True, stop=True), reads=("colsc2", "identf", "psF"), writes=("psF",))
                    O("dve", lambda e: e.tensor_copy(out=rowE[0:1, 0:256], in_=psF[0:1, 0:256]), reads=("psF", "rowE"), writes=("rowE",))
                    O("pe", lambda e: e.matmul(psA[:, 0:256], onesrow[:], rowE[0:1, 0:256], start=True, stop=True), reads=("rowE", "onesrow", "psA"), writes=("psA",))
                    O("dve", lambda e: e.tensor_scalar(out=betb[:], in0=psA[:, 0:256], scalar1=mwa[:, 1:2], scalar2=None, op0=ALU.mult), reads=("psA", "mwa", "betb"), writes=("betb",))
                    O("dve", lambda e: e.reciprocal(out=colsc2[:, 2:4], in_=colsc2[:, 0:2]), reads=("colsc2",), writes=("colsc2",))
                    for ti in range(2):
                        O("pe", lambda e, ti=ti: e.matmul(psF[0:1, ti * 128:(ti + 1) * 128], colsc2[:, 2 + ti:3 + ti], identf[:], start=True, stop=True), reads=("colsc2", "identf", "psF"), writes=("psF",))
                    O("dve", lambda e: e.tensor_copy(out=rowC[0:1, 0:256], in_=psF[0:1, 0:256]), reads=("psF", "rowCs"), writes=("rowCs",))
                    O("dve", lambda e: e.tensor_copy(out=rowC[0:1, 256:512], in_=psF[0:1, 0:256]), reads=("psF", "rowCs"), writes=("rowCs",))
                    O("pe", lambda e: e.matmul(psA[:, 0:512], c127row[:], rowC[0:1, 0:512], start=True, stop=True), reads=("rowCs", "c127row", "psA"), writes=("psA",))
                    O("dve", lambda e: e.tensor_copy(out=s2b2[:], in_=psA[:, 0:512]), reads=("psA", "s2b2"), writes=("s2b2",))
                    for jj in range(16):
                        p2_ = jj % 2
                        eng = "pool" if jj % 2 == 0 else "dve"
                        O(eng, lambda e, jj=jj, p2_=p2_: e.tensor_tensor(out=t32[p2_][:], in0=uS[:, jj * 512:(jj + 1) * 512], in1=s2b2[:], op=ALU.mult), reads=(f"uS{2 * jj}", f"uS{2 * jj + 1}", "s2b2", f"t32_{p2_}"), writes=(f"t32_{p2_}",))
                        rne_round(t32[p2_][:], (f"t32_{p2_}",), k2S[:, jj * 512:(jj + 1) * 512], f"k2j{jj}")

                def emit_mm2(g):
                    g0, lo, hi, n, goff = geo(g)
                    for fc2 in range(32):
                        p = fc2 % 4
                        O("sp", lambda e, fc2=fc2, p=p: e.dma_start(out=w2t_[p][:], in_=d_w2q[fc2 * 128:(fc2 + 1) * 128, :]), reads=(f"w2q_dram{fc2}", f"w2t{p}"), writes=(f"w2t{p}",), sem=f"d{p}", inc=16)
                        for mc in range(8):
                            O("pe", lambda e, p=p, mc=mc, fc2=fc2: e.matmul(acc_ps[mc][:, acc_off[mc]:acc_off[mc] + 256], w2t_[p][:, mc * 128:(mc + 1) * 128], k2S[:, fc2 * 256:(fc2 + 1) * 256], start=(fc2 == 0 and mc % 2 == 0), stop=(fc2 == 31 and mc % 2 == 1)), reads=(f"w2t{p}", f"k2j{fc2 // 2}", acc_nm[mc]), writes=(acc_nm[mc],), inc=(1 if mc == 7 or fc2 == 31 else 0))
                    for mc in range(8):
                        p = mc % 2
                        O("dve", lambda e, mc=mc, p=p: e.tensor_tensor(out=tmpe[p][:], in0=acc_ps[mc][:, acc_off[mc]:acc_off[mc] + 256], in1=betb[:], op=ALU.mult), reads=(acc_nm[mc], "betb", f"tmpe{p}"), writes=(f"tmpe{p}",))
                        O("pool", lambda e, mc=mc, g0=g0, p=p: e.dma_start(out=dst_dram[mc * 128:(mc + 1) * 128, g0:g0 + GRP], in_=tmpe[p][:], accum_op=ALU.add), reads=(f"tmpe{p}", dst_buf), writes=(dst_buf,), sem="wr", inc=16)

                PRE = 6
                emit_abg(0)
                emit_body(0, 0, 32)
                for g in range(NG):
                    emit_tail(g)
                    if g + 1 < NG:
                        emit_abg(g + 1)
                        emit_body(g + 1, 0, PRE)
                    emit_mm2(g)
                    if g + 1 < NG:
                        emit_body(g + 1, PRE, 32)

        def emit_attn():
            with contextlib.ExitStack() as at:
                zb = lambda n, sh, dt: at.enter_context(nc.sbuf_tensor(_un(n), sh, dt))
                yn = zb("yn", [128, 8 * T], BF16)
                oall = zb("oall", [128, 8 * T], BF16)
                cs2 = zb("cs2", [128, T], BF16)
                sc2 = zb("sc2", [128, T], BF16)
                qraw = [zb("qraw0", [128, T], BF16), zb("qraw1", [128, T], BF16)]
                kraw = [zb("kraw0", [128, T], BF16), zb("kraw1", [128, T], BF16)]
                qnbf = [zb("qnbf0", [128, T], BF16), zb("qnbf1", [128, T], BF16)]
                qkrot = [zb("qkrot0", [128, 2 * T], BF16), zb("qkrot1", [128, 2 * T], BF16)]
                pa = zb("pa", [128, T], BF16)
                pb = zb("pb", [128, T], BF16)
                vtm = [zb("vtm0", [128, T], BF16), zb("vtm1", [128, T], BF16)]
                wabf = [zb("wabf0", [128, DM], BF16), zb("wabf1", [128, DM], BF16)]
                pblk = [zb("pblk0", [128, 512], BF16), zb("pblk1", [128, 512], BF16)]
                nrow = zb("nrow", [1, 512], F32)
                nb = zb("nb", [128, 512], F32)
                tsc0 = zb("tsc0", [128, 512], F32)
                tsc1 = zb("tsc1", [128, 512], F32)
                mb2 = zb("mb2", [128, 512], F32)
                rstd2 = zb("rstd2", [128, 512], F32)
                xbd = [zb("xbda", [128, 512], BF16), zb("xbdb", [128, 512], BF16)]
                xqd = [zb("xqda", [128, 512], BF16), zb("xqdb", [128, 512], BF16)]
                sqb = zb("sqb", [128, 512], BF16)
                rqs = [zb("rqs0", [128, 16], F32), zb("rqs1", [128, 16], F32)]
                rks = [zb("rks0", [128, 16], F32), zb("rks1", [128, 16], F32)]
                tmpo = [zb("tmpo0", [128, 512], F32), zb("tmpo1", [128, 512], F32)]
                xw = [zb("xw0", [128, 8 * 512], F32), zb("xw1", [128, 8 * 512], F32)]

                for tab, ph, phL_, phb, tb in ((cs2, phc, phcL, "phc", "cs2"), (sc2, phs, phsL, "phs", "sc2")):
                    for sl in range(4):
                        O("pool", lambda e, sl=sl: e.iota(out=rowE[:], pattern=[[1, 512]], base=sl * 512, channel_multiplier=0, allow_small_or_imprecise_dtypes=True), reads=("rowE",), writes=("rowE",))
                        O("pe", lambda e: e.matmul(psA[:, 0:512], freq2[:], rowE[:], start=True, stop=True), reads=("freq2", "rowE", "psA"), writes=("psA",))
                        O("dve", lambda e, ph=ph, phL_=phL_, phb=phb: e.tensor_scalar(out=tsc0[:], in0=psA[:, 0:512], scalar1=ph[:], scalar2=phL_[:], op0=ALU.add, op1=ALU.add), reads=("psA", phb, phb + "L", "tsc0"), writes=("tsc0",))
                        O("dve", lambda e: e.tensor_scalar(out=tsc1[:], in0=tsc0[:], scalar1=1.0 / TWO_PI, scalar2=None, op0=ALU.mult), reads=("tsc0", "tsc1"), writes=("tsc1",))
                        O("dve", lambda e: e.tensor_scalar(out=tsc1[:], in0=tsc1[:], scalar1=mcol[:], scalar2=nmcol[:], op0=ALU.add, op1=ALU.add), reads=("tsc1", "mcol", "nmcol"), writes=("tsc1",))
                        O("dve", lambda e: e.scalar_tensor_tensor(out=tsc0[:], in0=tsc1[:], scalar=TWO_PI, in1=tsc0[:], op0=ALU.mult, op1=ALU.subtract), reads=("tsc1", "tsc0"), writes=("tsc0",))
                        O("dve", lambda e: e.tensor_scalar(out=tsc0[:], in0=tsc0[:], scalar1=-CLAMP, scalar2=CLAMP, op0=ALU.max, op1=ALU.min), reads=("tsc0",), writes=("tsc0",))
                        O("act", lambda e, tab=tab, sl=sl, tb=tb: e.activation(out=tab[:, sl * 512:(sl + 1) * 512], in_=tsc0[:], func=AF.Sin, scale=m1col[:]), reads=("tsc0", "m1col", tb), writes=(tb,))

                for tg in range(4):
                    t0 = tg * 512
                    wp = tg % 2
                    for fc in range(8):
                        O("sp", lambda e, fc=fc, t0=t0, wp=wp: e.dma_start(out=xw[wp][:, fc * 512:(fc + 1) * 512], in_=d_xwork[fc * 128:(fc + 1) * 128, t0:t0 + 512]), reads=("xwork", f"xw{wp}_{fc}"), writes=(f"xw{wp}_{fc}",), sem=f"d{fc % 4}", inc=16)
                    for fc in range(8):
                        xsl = xw[wp][:, fc * 512:(fc + 1) * 512]
                        p_ = fc % 2
                        xbl = xbd[p_]
                        xql = xqd[p_]
                        O("act", lambda e, xsl=xsl, xbl=xbl, p_=p_, fc=fc, wp=wp: e.activation(out=xbl[:], in_=xsl, func=AF.Copy), reads=(f"xw{wp}_{fc}", f"xbd{p_}"), writes=(f"xbd{p_}",))
                        O("act", lambda e, xbl=xbl, xql=xql, p_=p_: e.activation(out=xql[:], in_=xbl[:], func=AF.Square), reads=(f"xbd{p_}", f"xqd{p_}"), writes=(f"xqd{p_}",))
                        O("pe", lambda e, fc=fc, xbl=xbl, p_=p_: e.matmul(psE[:, 0:512], ones_bf[:], xbl[:], start=(fc == 0), stop=(fc == 7)), reads=(f"xbd{p_}", "ones_bf", "psE"), writes=("psE",))
                        O("pe", lambda e, fc=fc, xql=xql, p_=p_: e.matmul(psF[:, 0:512], ones_bf[:], xql[:], start=(fc == 0), stop=(fc == 7)), reads=(f"xqd{p_}", "ones_bf", "psF"), writes=("psF",))
                    O("dve", lambda e: e.tensor_scalar(out=mb2[:], in0=psE[:, 0:512], scalar1=1.0 / DM, scalar2=None, op0=ALU.mult), reads=("psE", "mb2"), writes=("mb2",))
                    O("dve", lambda e: e.tensor_mul(out=tsc0[:], in0=mb2[:], in1=mb2[:]), reads=("mb2", "tsc0"), writes=("tsc0",))
                    O("dve", lambda e: e.scalar_tensor_tensor(out=tsc0[:], in0=psF[:, 0:512], scalar=1.0 / DM, in1=tsc0[:], op0=ALU.mult, op1=ALU.subtract), reads=("psF", "tsc0"), writes=("tsc0",))
                    O("act", lambda e: e.activation(out=tsc0[:], in_=tsc0[:], func=AF.Sqrt, bias=eps5[:]), reads=("tsc0", "eps5"), writes=("tsc0",))
                    O("dve", lambda e: e.reciprocal(out=rstd2[:], in_=tsc0[:]), reads=("tsc0", "rstd2"), writes=("rstd2",))
                    for fc in range(8):
                        xsl = xw[wp][:, fc * 512:(fc + 1) * 512]
                        O("dve", lambda e, xsl=xsl, fc=fc, wp=wp: e.tensor_sub(out=tsc1[:], in0=xsl, in1=mb2[:]), reads=(f"xw{wp}_{fc}", "mb2", "tsc1"), writes=("tsc1",))
                        O("dve", lambda e, fc=fc, t0=t0: e.tensor_tensor(out=yn[:, fc * T + t0: fc * T + t0 + 512], in0=tsc1[:], in1=rstd2[:], op=ALU.mult), reads=("tsc1", "rstd2", "yn"), writes=("yn",))

                def proj(dsrc, dbuf, hidx, dst_ap, dst_buf, slot):
                    O("pool", lambda e, hidx=hidx, slot=slot, dsrc=dsrc: e.dma_start(out=wabf[slot][:].rearrange("p (c d) -> p c d", c=8), in_=dsrc[:, hidx * 128:(hidx + 1) * 128].rearrange("(c p) d -> p c d", p=128)), reads=(f"wabf{slot}",), writes=(f"wabf{slot}",), sem=("wa" if slot == 0 else "wb"), inc=16)
                    for tg in range(4):
                        psum, psb = (psA, "psA") if tg % 2 == 0 else (psB, "psB")
                        for fc in range(8):
                            O("pe", lambda e, fc=fc, tg=tg, slot=slot, psum=psum: e.matmul(psum[:, 0:512], wabf[slot][:, fc * 128:(fc + 1) * 128], yn[:, fc * T + tg * 512: fc * T + (tg + 1) * 512], start=(fc == 0), stop=(fc == 7)), reads=(f"wabf{slot}", "yn", psb), writes=(psb,), inc=(1 if fc == 7 else 0))
                        O("dve", lambda e, tg=tg, psum=psum, psb=psb: e.tensor_copy(out=dst_ap[:, tg * 512:(tg + 1) * 512], in_=psum[:, 0:512]), reads=(psb, dst_buf), writes=(dst_buf,))

                QO = [288, 320]
                KO = [256, 272]

                def stage_pq(h):
                    pi = h % 2
                    proj(d_wq, "wq", h, qraw[pi], f"qraw{pi}", pi)
                    for tg in range(4):
                        O("act", lambda e, tg=tg, pi=pi: e.activation(out=sqb[:], in_=qraw[pi][:, tg * 512:(tg + 1) * 512], func=AF.Square), reads=(f"qraw{pi}", "sqb"), writes=("sqb",))
                        for ti in range(4):
                            O("pe", lambda e, tg=tg, ti=ti, pi=pi: e.matmul(psS[:, QO[pi] + tg * 4 + ti: QO[pi] + tg * 4 + ti + 1], sqb[:, ti * 128:(ti + 1) * 128], onescol_bf[:], start=True, stop=True), reads=("sqb", "onescol", f"psSq{pi}"), writes=(f"psSq{pi}",))

                def stage_pq2(h):
                    pi = h % 2
                    O("act", lambda e, pi=pi: e.activation(out=rqs[pi][:], in_=psS[:, QO[pi]:QO[pi] + 16], func=AF.Sqrt, scale=c128[:], bias=eps6[:]), reads=(f"psSq{pi}", "c128", "eps6", f"rqs{pi}"), writes=(f"rqs{pi}",))
                    O("dve", lambda e, pi=pi: e.reciprocal(out=rqs[pi][:], in_=rqs[pi][:]), reads=(f"rqs{pi}",), writes=(f"rqs{pi}",))
                    for tg in range(4):
                        for ti in range(4):
                            O("pe", lambda e, tg=tg, ti=ti, pi=pi: e.matmul(psF[0:1, ti * 128:(ti + 1) * 128], rqs[pi][:, tg * 4 + ti: tg * 4 + ti + 1], identf[:], start=True, stop=True), reads=(f"rqs{pi}", "identf", "psF"), writes=("psF",))
                        O("dve", lambda e: e.tensor_copy(out=rowE[0:1, 0:512], in_=psF[0:1, 0:512]), reads=("psF", "rowE"), writes=("rowE",))
                        O("pe", lambda e: e.matmul(psB[:, 0:512], onesrow[:], rowE[0:1, 0:512], start=True, stop=True), reads=("rowE", "onesrow", "psB"), writes=("psB",))
                        O("dve", lambda e, tg=tg, pi=pi: e.tensor_tensor(out=qnbf[pi][:, tg * 512:(tg + 1) * 512], in0=qraw[pi][:, tg * 512:(tg + 1) * 512], in1=psB[:, 0:512], op=ALU.mult), reads=(f"qraw{pi}", "psB", f"qnbf{pi}"), writes=(f"qnbf{pi}",))
                    O("dve", lambda e, pi=pi: e.tensor_tensor(out=pa[:], in0=qnbf[pi][:], in1=cs2[:], op=ALU.mult), reads=(f"qnbf{pi}", "cs2", "pa"), writes=("pa",))
                    O("pool", lambda e, pi=pi: e.tensor_tensor(out=pb[:], in0=qnbf[pi][:], in1=sc2[:], op=ALU.mult), reads=(f"qnbf{pi}", "sc2", "pb"), writes=("pb",))
                    for sl4 in range(4):
                        O("pe", lambda e, sl4=sl4: e.matmul(psA[:, 0:512], selE[:], pa[:, sl4 * 512:(sl4 + 1) * 512], start=True, stop=False), reads=("selE", "pa", "psA"), writes=("psA",), inc=0)
                        O("pe", lambda e, sl4=sl4: e.matmul(psA[:, 0:512], selO[:], pb[:, sl4 * 512:(sl4 + 1) * 512], start=False, stop=True), reads=("selO", "pb", "psA"), writes=("psA",))
                        O("dve", lambda e, sl4=sl4, pi=pi: e.tensor_copy(out=qkrot[pi][:, sl4 * 512:(sl4 + 1) * 512], in_=psA[:, 0:512]), reads=("psA", f"qkrot{pi}"), writes=(f"qkrot{pi}",))

                def stage_pk(h):
                    pi = h % 2
                    proj(d_wk, "wk", h, kraw[pi], f"kraw{pi}", (h + 1) % 2)
                    for tg in range(4):
                        O("act", lambda e, tg=tg, pi=pi: e.activation(out=sqb[:], in_=kraw[pi][:, tg * 512:(tg + 1) * 512], func=AF.Square), reads=(f"kraw{pi}", "sqb"), writes=("sqb",))
                        for ti in range(4):
                            O("pe", lambda e, tg=tg, ti=ti, pi=pi: e.matmul(psS[:, KO[pi] + tg * 4 + ti: KO[pi] + tg * 4 + ti + 1], sqb[:, ti * 128:(ti + 1) * 128], onescol_bf[:], start=True, stop=True), reads=("sqb", "onescol", f"psSc{pi}"), writes=(f"psSc{pi}",))

                def stage_pk2(h):
                    pi = h % 2
                    O("act", lambda e, pi=pi: e.activation(out=rks[pi][:], in_=psS[:, KO[pi]:KO[pi] + 16], func=AF.Sqrt, scale=c128[:], bias=eps6[:]), reads=(f"psSc{pi}", "c128", "eps6", f"rks{pi}"), writes=(f"rks{pi}",))
                    O("dve", lambda e, pi=pi: e.reciprocal(out=rks[pi][:], in_=rks[pi][:]), reads=(f"rks{pi}",), writes=(f"rks{pi}",))
                    O("dve", lambda e, pi=pi: e.tensor_scalar(out=rks[pi][:], in0=rks[pi][:], scalar1=1.0 / 128.0, scalar2=None, op0=ALU.mult), reads=(f"rks{pi}",), writes=(f"rks{pi}",))
                    O("dve", lambda e, pi=pi: e.tensor_tensor(out=pa[:], in0=kraw[pi][:], in1=cs2[:], op=ALU.mult), reads=(f"kraw{pi}", "cs2", "pa"), writes=("pa",))
                    O("pool", lambda e, pi=pi: e.tensor_tensor(out=pb[:], in0=kraw[pi][:], in1=sc2[:], op=ALU.mult), reads=(f"kraw{pi}", "sc2", "pb"), writes=("pb",))
                    for sl4 in range(4):
                        O("pe", lambda e, sl4=sl4: e.matmul(psA[:, 0:512], selE[:], pa[:, sl4 * 512:(sl4 + 1) * 512], start=True, stop=False), reads=("selE", "pa", "psA"), writes=("psA",), inc=0)
                        O("pe", lambda e, sl4=sl4: e.matmul(psA[:, 0:512], selO[:], pb[:, sl4 * 512:(sl4 + 1) * 512], start=False, stop=True), reads=("selO", "pb", "psA"), writes=("psA",))
                        O("dve", lambda e, sl4=sl4, pi=pi: e.tensor_copy(out=qkrot[pi][:, T + sl4 * 512: T + (sl4 + 1) * 512], in_=psA[:, 0:512]), reads=("psA", f"qkrot{pi}"), writes=(f"qkrot{pi}",))

                def stage_pv(h):
                    pi = h % 2
                    proj(d_wv, "wv", h, qnbf[pi], f"qnbf{pi}", h % 2)
                    for tc in range(16):
                        O("sp", lambda e, tc=tc, pi=pi: e.dma_start_transpose(out=vtm[pi][:, tc * 128:(tc + 1) * 128], in_=qnbf[pi][:, tc * 128:(tc + 1) * 128]), reads=(f"qnbf{pi}", f"vtm{pi}"), writes=(f"vtm{pi}",), sem="ds", inc=16)
                    em.join((f"vtm{pi}",))

                def emit_sc(h, ig):
                    pi = h % 2
                    njc = 4 * ig + 4

                    def sops(jc):
                        sp_, spb = (psC, "psC") if jc % 2 == 0 else (psD, "psD")
                        pp = jc % 2
                        O("pe", lambda e, jc=jc, ig=ig, sp_=sp_, pi=pi: e.matmul(sp_[:, 0:512], qkrot[pi][:, T + jc * 128: T + (jc + 1) * 128], qkrot[pi][:, ig * 512:(ig + 1) * 512], start=True, stop=True), reads=(f"qkrot{pi}", spb), writes=(spb,))
                        O("act", lambda e, sp_=sp_, jc=jc, pp=pp, pi=pi: e.activation(out=pblk[pp][:], in_=sp_[:, 0:512], func=AF.Exp, scale=rks[pi][:, jc:jc + 1]), reads=(spb, f"rks{pi}", f"pblk{pp}"), writes=(f"pblk{pp}",))
                        if jc >= 4 * ig:
                            off = jc * 128 - ig * 512
                            if off > 0:
                                O("dve", lambda e, pp=pp, off=off: e.memset(pblk[pp][:, 0:off], 0.0), reads=(f"pblk{pp}",), writes=(f"pblk{pp}",))
                            O("dve", lambda e, pp=pp, off=off: e.tensor_tensor(out=pblk[pp][:, off:off + 128], in0=pblk[pp][:, off:off + 128], in1=masktb[:], op=ALU.mult), reads=(f"pblk{pp}", "masktb"), writes=(f"pblk{pp}",))

                    def oops(jc):
                        pp = jc % 2
                        O("pe", lambda e, jc=jc, pp=pp, njc=njc, pi=pi: e.matmul(psE[:, 0:512], vtm[pi][:, jc * 128:(jc + 1) * 128], pblk[pp][:], start=(jc == 0), stop=(jc == njc - 1)), reads=(f"vtm{pi}", f"pblk{pp}", "psE"), writes=("psE",), inc=(1 if jc == njc - 1 else 0))
                        O("pe", lambda e, jc=jc, pp=pp, njc=njc: e.matmul(psF[0:1, 0:512], onescol_bf[:], pblk[pp][:], start=(jc == 0), stop=(jc == njc - 1)), reads=("onescol", f"pblk{pp}", "psF"), writes=("psF",), inc=(1 if jc == njc - 1 else 0))

                    sops(0)
                    for jc in range(1, njc):
                        sops(jc)
                        oops(jc - 1)
                    oops(njc - 1)
                    O("dve", lambda e: e.reciprocal(out=nrow[:], in_=psF[0:1, 0:512]), reads=("psF", "nrow"), writes=("nrow",))
                    O("pe", lambda e: e.matmul(psF[:, 0:512], onesrow[:], nrow[:], start=True, stop=True), reads=("nrow", "onesrow", "psF"), writes=("psF",))
                    O("dve", lambda e: e.tensor_copy(out=nb[:], in_=psF[:, 0:512]), reads=("psF", "nb"), writes=("nb",))
                    O("dve", lambda e, h=h, ig=ig: e.tensor_tensor(out=oall[:, h * T + ig * 512: h * T + (ig + 1) * 512], in0=psE[:, 0:512], in1=nb[:], op=ALU.mult), reads=("psE", "nb", "oall"), writes=("oall",))

                stage_pq(0)
                stage_pq2(0)
                stage_pk(0)
                stage_pk2(0)
                stage_pv(0)
                for h in range(H):
                    for ig in range(4):
                        emit_sc(h, ig)
                        if h + 1 < H:
                            if ig == 0:
                                stage_pq(h + 1)
                            elif ig == 1:
                                stage_pk(h + 1)
                            elif ig == 2:
                                stage_pq2(h + 1)
                                stage_pk2(h + 1)
                            else:
                                stage_pv(h + 1)

                for mc in range(8):
                    slot = mc % 2
                    O("pool", lambda e, mc=mc, slot=slot: e.dma_start(out=wabf[slot][:].rearrange("p (c d) -> p c d", c=8), in_=d_wo[:, mc * 128:(mc + 1) * 128].rearrange("(c p) d -> p c d", p=128)), reads=(f"wabf{slot}",), writes=(f"wabf{slot}",), sem=("wa" if slot == 0 else "wb"), inc=16)
                    for tg in range(4):
                        psum, psb = (psA, "psA") if tg % 2 == 0 else (psB, "psB")
                        p_ = tg % 2
                        for hh in range(8):
                            O("pe", lambda e, hh=hh, tg=tg, slot=slot, psum=psum, psb=psb: e.matmul(psum[:, 0:512], wabf[slot][:, hh * 128:(hh + 1) * 128], oall[:, hh * T + tg * 512: hh * T + (tg + 1) * 512], start=(hh == 0), stop=(hh == 7)), reads=(f"wabf{slot}", "oall", psb), writes=(psb,), inc=(1 if hh == 7 else 0))
                        O("dve", lambda e, psum=psum, psb=psb, p_=p_: e.tensor_copy(out=tmpo[p_][:], in_=psum[:, 0:512]), reads=(psb, f"tmpo{p_}"), writes=(f"tmpo{p_}",))
                        O("pool", lambda e, mc=mc, tg=tg, p_=p_: e.dma_start(out=d_xwork[mc * 128:(mc + 1) * 128, tg * 512:(tg + 1) * 512], in_=tmpo[p_][:], accum_op=ALU.add), reads=(f"tmpo{p_}", "xwork"), writes=("xwork",), sem="wr", inc=16)

        done = [False]

        def _stop(stage):
            if done[0]:
                return True
            if upto == stage:
                done[0] = True
            return False

        with contextlib.ExitStack() as f1:
            k1S_1 = f1.enter_context(nc.sbuf_tensor("k1S_1", [128, 8 * T], BF16))
            arow_1 = f1.enter_context(nc.sbuf_tensor("arow_1", [1, T], F32))
            emit_W()
            em.barrier()
            _stop("W")
            if upto.startswith("f1a:"):
                done[0] = False
            if not done[0]:
                lim = upto.split(":", 1)[1] if upto.startswith("f1a:") else ""
                emit_sub0(k1S_1, arow_1, d_x, None, copy_to=d_xwork, limit=lim)
                em.barrier()
                if upto.startswith("f1a"):
                    done[0] = True
            if not done[0]:
                emit_sub12(k1S_1, arow_1, d_xwork, "xwork")
                em.barrier()
                _stop("f1b")
        for fc in range(8):
            em.buf.pop(f"xs{fc}", None)
        if not done[0]:
            emit_attn()
            em.barrier()
            _stop("attn")
        with contextlib.ExitStack() as f2:
            k1S_2 = f2.enter_context(nc.sbuf_tensor("k1S_2", [128, 8 * T], BF16))
            arow_2 = f2.enter_context(nc.sbuf_tensor("arow_2", [1, T], F32))
            for fc in range(8):
                em.buf.pop(f"xs{fc}", None)
            if not done[0]:
                emit_sub0(k1S_2, arow_2, d_xwork, "xwork", copy_to=d_out, copy_buf="out_dram")
                em.barrier()
                _stop("f2a")
            if not done[0]:
                emit_sub12(k1S_2, arow_2, d_out, "out_dram")

        em.prog["sp"].append(([(s, em.cnt[s]) for s in ("wr", "ds") if em.cnt.get(s, 0) > 0], None, None, 0, None))
        em.finalize_check()

        eng_sems = [x for x in em.cnt if x not in DMA_SEMS]
        referenced = {x: set() for x in eng_sems}
        for e in ENG2SECT:
            for waits, _, _, _, _ in em.prog[e]:
                for s_, c_ in waits:
                    if s_ in referenced:
                        referenced[s_].add(c_)
        rank = {}
        for s_ in eng_sems:
            rank[s_] = {c_: i + 1 for i, c_ in enumerate(sorted(referenced[s_]))}

        with nc.Block() as block:
            def runner(eng_name):
                def go(handle):
                    for waits, fn, semn, inc, mycnt in em.prog[eng_name]:
                        for s, c in waits:
                            handle.wait_ge(semobjs[s], rank[s][c] if s in rank else c)
                        if fn is not None:
                            inst = fn(handle)
                            if semn is not None and (semn in DMA_SEMS or (mycnt is not None and mycnt[1] in referenced.get(semn, ()))):
                                inst.then_inc(semobjs[semn], inc)
                return go

            block.sync(runner("sp"))
            block.vector(runner("dve"))
            block.scalar(runner("act"))
            block.gpsimd(runner("pool"))
            block.tensor(runner("pe"))

    return nc


_NC_CACHE = [None]


def kernel(L0_attn_x, L0_attn_wq, L0_attn_wk, L0_attn_wv, L0_attn_wo,
           L0_ffn_w1, L0_ffn_conv_w, L0_ffn_conv_b, L0_ffn_w2):
    x = np.asarray(L0_attn_x, np.float32)
    wq = np.ascontiguousarray(np.asarray(L0_attn_wq, np.float32).reshape(DM, DM))
    wk = np.ascontiguousarray(np.asarray(L0_attn_wk, np.float32).reshape(DM, DM))
    wv = np.ascontiguousarray(np.asarray(L0_attn_wv, np.float32).reshape(DM, DM))
    wo = np.ascontiguousarray(np.asarray(L0_attn_wo, np.float32).reshape(DM, DM))
    w1t = np.ascontiguousarray(np.asarray(L0_ffn_w1, np.float32).T)
    w2t = np.ascontiguousarray(np.asarray(L0_ffn_w2, np.float32).T)
    cw = np.ascontiguousarray(np.asarray(L0_ffn_conv_w, np.float32)[:, 0, :]
                              .reshape(32, 128, 3).transpose(1, 0, 2).reshape(128, 96))
    cb = np.ascontiguousarray(np.asarray(L0_ffn_conv_b, np.float32).reshape(32, 128).T)

    if _NC_CACHE[0] is None:
        _NC_CACHE[0] = build()
    nc = _NC_CACHE[0]
    in_maps = []
    for b in range(8):
        in_maps.append({
            "x": np.ascontiguousarray(x[b].T),
            "wq": wq, "wk": wk, "wv": wv, "wo": wo,
            "w1t": w1t, "w2t": w2t, "cw": cw, "cb": cb,
        })
    res = run_bass_kernel_spmd(nc, in_maps, core_ids=list(range(8)))
    out = np.stack([np.ascontiguousarray(np.asarray(r["out"], np.float32).T) for r in res.results])
    return out
